# version 13
# baseline (speedup 1.0000x reference)
import contextlib
import numpy as np
import concourse.bass as bass
import concourse.mybir as mybir
from concourse.bass_utils import run_bass_kernel_spmd

F32 = mybir.dt.float32
BF16 = mybir.dt.bfloat16
AF = mybir.ActivationFunctionType
ALU = mybir.AluOpType
AX = mybir.AxisListType

SEMNAMES = ("CQ", "XQ", "XB", "XC", "XD", "XE", "TS", "VS", "SS", "W0", "W1", "W2", "W3", "W4", "W5", "W6")
ENGS = ("tensor", "vector", "scalar", "sync", "gpsimd")


def real_cfg():
    return dict(D=4096, S=1024, FF=9984, E=8, R=16, TOKENS=8192, NCORES=8, TP=512,
                THD=2.5, EPS=1e-5, G=6)


def derive(cfg):
    c = dict(cfg)
    c["KC"] = c["D"] // 128
    c["SC"] = c["S"] // 128
    c["FC"] = c["FF"] // 128
    c["OC"] = c["D"] // 128
    assert c["E"] * c["R"] == 128
    secs, i = [], 0
    while i < c["FC"]:
        n = min(20, c["FC"] - i)
        secs.append(list(range(i, i + n)))
        i += n
    c["SECS"] = secs
    c["VSEC"] = max(len(s) for s in secs)
    c["NPASS"] = c["TOKENS"] // c["NCORES"] // c["TP"]
    c["TPC"] = c["TOKENS"] // c["NCORES"]
    c["NT"] = c["TP"] // 128
    return c


def make_plan(c):
    plan = []

    def add(kind, sel, cols, carve):
        plan.append({"kind": kind, "sel": sel, "cols": cols, "carve": carve})

    SH = c["SC"] // 2
    for h in range(2):
        for k in range(c["KC"]):
            add("upT_row", (k, h), SH * 128, "A")
    for h in range(2):
        for k in range(c["KC"]):
            add("gateT_row", (k, h), SH * 128, "A")
    for m in range(7):
        for o in range(c["OC"]):
            add("sterm", (m, o), c["SC"] * 128, "A")
    FH = c["KC"] // 2
    FQ = c["KC"] // 4
    for si, sec in enumerate(c["SECS"]):
        for f in sec:
            for gu in ("g", "u"):
                for q in range(4):
                    add("fgu", (gu, f, q), FQ * 128, "A")
        runs = []
        i0 = 0
        while i0 < len(sec):
            n = min(8, len(sec) - i0)
            runs.append((i0, n))
            i0 += n
        for o in range(c["OC"]):
            for ri, (r0, rn) in enumerate(runs):
                add("down", (si, o, ri, r0, rn), rn * 128, "A")
    add("router", None, c["KC"] * 8, "A")
    for q in range(4):
        add("lA", q, FQ * 128, "A")
    for o in range(c["OC"]):
        for q in range(4):
            add("b2", (o, q), FQ * 128, "A")
    for q in range(4):
        add("lorab", q, (c["OC"] // 4) * 128, "A")
    off = 0
    for p in plan:
        p["off"] = off
        off += p["cols"]
    return plan, off


def _img_tiles(block):
    n = block.shape[0] // 128
    w = block.shape[1]
    return np.ascontiguousarray(
        block.reshape(n, 128, w).transpose(1, 0, 2).reshape(128, n * w))


def build_stream(c, plan, total_cols, W):
    out = np.empty((128, total_cols), np.float32)
    for p in plan:
        k, sel, off, cols = p["kind"], p["sel"], p["off"], p["cols"]
        if k in ("upT_row", "gateT_row"):
            kk, h = sel
            srcw = W["upT"] if k == "upT_row" else W["gateT"]
            SH = srcw.shape[1] // 256
            img = srcw[kk * 128:(kk + 1) * 128, h * SH * 128:(h + 1) * SH * 128]
        elif k == "sterm":
            m, o = sel
            src = W["laoT"] if m == 0 else W["fuseT"][m - 1]
            img = _img_tiles(src[:, o * 128:(o + 1) * 128])
        elif k == "fgu":
            gu, f, q = sel
            srcw = W["fgT"] if gu == "g" else W["fuT"]
            FQ = srcw.shape[0] // 512
            img = _img_tiles(srcw[q * FQ * 128:(q + 1) * FQ * 128,
                                  f * 128:(f + 1) * 128])
        elif k == "down":
            si, o, ri, r0, rn = sel
            sec = c["SECS"][si]
            f0 = sec[0] + r0
            img = _img_tiles(W["fdT"][f0 * 128:(f0 + rn) * 128,
                                      o * 128:(o + 1) * 128])
        elif k == "router":
            img = _img_tiles(W["wrT"])
        elif k == "lA":
            q = sel
            FQ = W["lAT"].shape[0] // 512
            img = _img_tiles(W["lAT"][q * FQ * 128:(q + 1) * FQ * 128, :])
        elif k == "b2":
            o, q = sel
            FQ = W["wbT"].shape[0] // 512
            img = _img_tiles(W["wbT"][q * FQ * 128:(q + 1) * FQ * 128,
                                      o * 128:(o + 1) * 128])
        elif k == "lorab":
            q = sel
            qw = W["lBT"].shape[1] // 4
            img = W["lBT"][:, q * qw:(q + 1) * qw]
        else:
            raise KeyError(k)
        assert img.shape == (128, cols), (k, img.shape, cols)
        out[:, off:off + cols] = img
    return out


class Buf:
    __slots__ = ("w", "rs")

    def __init__(self):
        self.w = None
        self.rs = []


class P:
    def __init__(self):
        self.prog = {e: [] for e in ENGS}
        self.cnt = dict.fromkeys(SEMNAMES, 0)
        self.wm = {(e, s): 0 for e in ENGS for s in SEMNAMES}
        self.R = {}

    def wait(self, eng, sem, val):
        if val is None or val <= 0 or self.wm[(eng, sem)] >= val:
            return
        self.wm[(eng, sem)] = val
        self.prog[eng].append(("w", sem, val))

    def waitt(self, eng, tok):
        if tok is not None:
            self.wait(eng, tok[0], tok[1])

    def buf(self, key):
        b = self.R.get(key)
        if b is None:
            b = self.R[key] = Buf()
        return b

    def pre(self, eng, reads=(), writes=()):
        need = {}
        def want(t):
            if t is not None and t[1] > need.get(t[0], 0):
                need[t[0]] = t[1]
        for key in reads:
            want(self.buf(key).w)
        for key in writes:
            b = self.buf(key)
            want(b.w)
            for t in b.rs:
                want(t)
        for sem, val in need.items():
            self.wait(eng, sem, val)

    def post(self, tok, reads=(), writes=(), const_reads=()):
        for key in writes:
            b = self.buf(key)
            b.w = tok
            b.rs = []
        for key in reads:
            self.buf(key).rs.append(tok)

    def emit(self, eng, fn, inc=None, n=1):
        tok = None
        if inc is not None:
            self.cnt[inc] += n
            tok = (inc, self.cnt[inc])
        self.prog[eng].append(("o", fn, inc, n))
        return tok

    def o(self, eng, fn, inc, reads=(), writes=(), n=1):
        self.pre(eng, reads, writes)
        tok = self.emit(eng, fn, inc, n)
        self.post(tok, reads, writes)
        return tok

    def v(self, fn, reads=(), writes=()):
        return self.o("vector", fn, "VS", reads, writes)

    def s(self, fn, reads=(), writes=()):
        return self.o("scalar", fn, "SS", reads, writes)


def build_core_program(cfg):
    c = derive(cfg)
    D, S, FF, NE, R = c["D"], c["S"], c["FF"], c["E"], c["R"]
    KC, SC, FC, OC, TP = c["KC"], c["SC"], c["FC"], c["OC"], c["TP"]
    NPASS, TPC, NT = c["NPASS"], c["TPC"], c["NT"]
    THD, EPS = c["THD"], c["EPS"]
    plan, total_cols = make_plan(c)

    FH = KC // 2
    FQ = KC // 4
    CARVE = {"A": (max(e["cols"] for e in plan), 6)}
    ARENA = max(sc * ns for sc, ns in CARVE.values())

    nc = bass.Bass("TRN2", target_bir_lowering=False)
    xT = nc.dram_tensor("xT", [D, TPC], F32, kind="ExternalInput").ap()
    ws = nc.dram_tensor("wstream", [128, total_cols], F32, kind="ExternalInput").ap()
    lp_h = nc.dram_tensor("lp_pack", [128, SC], F32, kind="ExternalInput").ap()
    b_h = nc.dram_tensor("b_pack", [128, OC], F32, kind="ExternalInput").ap()
    disc_h = nc.dram_tensor("disc_rep", [128, 6], F32, kind="ExternalInput").ap()
    rrep_h = nc.dram_tensor("rrep", [NE, 128], F32, kind="ExternalInput").ap()
    ident_h = nc.dram_tensor("ident", [128, 128], F32, kind="ExternalInput").ap()
    onescol_h = nc.dram_tensor("ones_col", [1, 128], F32, kind="ExternalInput").ap()
    outT = nc.dram_tensor("outT", [D, TPC], F32, kind="ExternalOutput").ap()

    st = contextlib.ExitStack()
    sb = lambda nm, shp, dt: st.enter_context(nc.sbuf_tensor(nm, shp, dt))
    with st:
        acc = sb("acc", [128, OC * TP], F32)
        xbf = sb("xbf", [128, KC * TP], BF16)
        aubf = sb("aubf", [128, SC * TP], BF16)
        delta = sb("delta", [128, SC * TP], F32)
        pw = sb("pw", [128, SC * TP], F32)
        tmpb = [sb("tmp0", [128, SC * TP], BF16), sb("tmp1", [128, SC * TP], BF16)]
        vsec = sb("vsec", [128, c["VSEC"] * TP], BF16)
        wring = sb("wring", [128, ARENA], BF16)
        lp_s = sb("lp_s", [128, SC], F32)
        lpn_s = sb("lpn_s", [128, SC], F32)
        b_s = sb("b_s", [128, OC], F32)
        crep = sb("crep", [128, 6], F32)
        rrep_s = sb("rrep_s", [NE, 128], F32)
        ident_s = sb("ident_s", [128, 128], F32)
        onescol_s = sb("ones_col_s", [1, 128], F32)
        onesk_s = sb("onesk_s", [128, 1], BF16)
        sg = [sb("sgA", [128, TP], F32), sb("sgB", [128, TP], F32)]
        statb = [sb("statA", [128, TP], BF16), sb("statB", [128, TP], BF16)]
        sqb = [sb("sqA", [128, TP], BF16), sb("sqB", [128, TP], BF16)]
        rows = sb("rows", [1, 3 * TP], F32)
        lnt = sb("lnt", [128, TP], F32)
        lt = sb("lt", [128, NT * NE], F32)
        t2a = sb("t2a", [128, NE], F32)
        t2b = sb("t2b", [128, NE], F32)
        t2c = sb("t2c", [128, 4], F32)
        wsb = sb("wsb", [NE, TP], F32)
        wrepsb = sb("wrepsb", [128, TP], F32)
        hbf = sb("hbf", [128, TP], BF16)
        pb = [st.enter_context(nc.psum_tensor("pb%d" % i, [128, TP], F32))
              for i in range(8)]
        SEM = {nm: st.enter_context(nc.semaphore(nm)) for nm in SEMNAMES}
        block = st.enter_context(nc.Block())

        p = P()
        ACCK = [("acc", o) for o in range(OC)]
        XBFK = [("xbf", k) for k in range(KC)]

        plan_by_carve_pos = {}
        cur_carve = [None]
        carve_keys = [set()]

        def dma_chunk(entry, pass_i):
            carve = entry["carve"]
            cols, nslot = CARVE[carve]
            if cur_carve[0] != carve:
                for key in carve_keys[0]:
                    b = p.buf(key)
                    p.waitt("gpsimd", b.w)
                    for t in b.rs:
                        p.waitt("gpsimd", t)
                    del p.R[key]
                cur_carve[0] = carve
                carve_keys[0] = set()
                plan_by_carve_pos[carve] = 0
            i = plan_by_carve_pos[carve]
            plan_by_carve_pos[carve] = i + 1
            slot = i % nslot
            so = slot * cols
            key = ("slot", carve, slot)
            carve_keys[0].add(key)
            ccols = entry["cols"]
            off = entry["off"]
            sem = "W%d" % slot
            tok = p.o("gpsimd",
                      lambda E, off=off, ccols=ccols, so=so:
                      E.dma_start(wring[:, so:so + ccols], ws[:, off:off + ccols]),
                      sem, writes=(key,), n=16)
            return tok, so, key

        def mm_group2(bank, chunks, extra_reads=()):
            p.pre("tensor", reads=(), writes=(("pb", bank),))
            tot = sum(len(it) for _, it in chunks)
            i = 0
            tok = None
            for dret, items in chunks:
                need = {}
                if dret is not None:
                    need[dret[0][0]] = max(need.get(dret[0][0], 0), dret[0][1])
                mov_keys = set()
                for sf, mf, mk in items:
                    if mk is not None:
                        mov_keys.add(mk)
                        t = p.buf(mk).w
                        if t is not None:
                            need[t[0]] = max(need.get(t[0], 0), t[1])
                for sem, val in need.items():
                    p.wait("tensor", sem, val)
                n_it = len(items)
                for j, (sf, mf, mk) in enumerate(items):
                    def fn(E, b=bank, sf=sf, mf=mf, st_=(i == 0), sp=(i == tot - 1)):
                        return E.matmul(pb[b][:, :], sf(), mf(), start=st_, stop=sp)
                    if j == n_it - 1:
                        tok = p.emit("tensor", fn, "TS")
                    else:
                        p.emit("tensor", fn, None)
                    i += 1
                rd = list(mov_keys)
                if dret is not None:
                    rd.append(dret[2])
                p.post(tok, reads=rd, writes=[])
            p.post(tok, reads=list(extra_reads), writes=[("pb", bank)])
            return tok

        const_keys = []
        for ap_s, ap_h, key in ((lp_s, lp_h, "lp"), (b_s, b_h, "b"),
                                (crep, disc_h, "crep"), (rrep_s, rrep_h, "rrep"),
                                (ident_s, ident_h, "ident"),
                                (onescol_s, onescol_h, "onescol")):
            p.emit("sync", lambda E, d=ap_s, s_=ap_h: E.dma_start(d[:, :], s_[:, :]),
                   "CQ", 16)
            const_keys.append(key)
        ctok = ("CQ", p.cnt["CQ"])
        for key in const_keys:
            p.buf(key).w = ctok
        p.s(lambda E: E.activation(crep[:, :], crep[:, :], AF.Exp, scale=-1.0),
            reads=("crep",), writes=("crep",))
        p.v(lambda E: E.tensor_scalar(lpn_s[:, :], lp_s[:, :], -1.0, None, ALU.mult),
            reads=("lp",), writes=("lpn",))
        p.v(lambda E: E.tensor_scalar(onesk_s[:, :], b_s[:, 0:1], 0.0, 1.0,
                                      ALU.mult, ALU.add),
            reads=("b",), writes=("onesk",))

        for ps in range(NPASS):
            t0 = ps * TP
            p.o("sync", lambda E, t0=t0: E.dma_start(
                acc.rearrange("p (kc t) -> p kc t", kc=OC),
                xT.rearrange("(kc p) t -> p kc t", p=128)[:, :, t0:t0 + TP]),
                "XQ", writes=ACCK, n=16)
            QK = KC // 4

            def emit_xbf_quarter(qi, t0=t0, QK=QK):
                qsem = ("XB", "XC", "XD", "XE")[qi]
                p.o("gpsimd", lambda E, t0=t0, QK=QK, qi=qi: E.dma_start(
                    xbf.rearrange("p (kc t) -> p kc t", kc=KC)[:, qi * QK:(qi + 1) * QK, :],
                    xT.rearrange("(kc p) t -> p kc t", p=128)[:, qi * QK:(qi + 1) * QK, t0:t0 + TP]),
                    qsem, writes=XBFK[qi * QK:(qi + 1) * QK], n=16)

            plan_i = [0]

            def next_entry(expect_kind=None):
                e = plan[plan_i[0]]
                plan_i[0] += 1
                if expect_kind is not None:
                    assert e["kind"] == expect_kind, (e, expect_kind)
                return e

            SH = SC // 2
            for sub, kind in (("a", "upT_row"), ("b", "gateT_row")):
                for h in range(2):
                    for k in range(KC):
                        if sub == "a" and h == 0 and k % QK == 0 and k // QK < 2:
                            emit_xbf_quarter(k // QK)
                        if sub == "a" and h == 0 and k >= 2 * QK and (k - 2 * QK) % 2 == 0:
                            for kcc in (k, k + 1):
                                if kcc < KC:
                                    p.s(lambda E, kcc=kcc: E.copy(
                                        xbf[:, kcc * TP:(kcc + 1) * TP],
                                        acc[:, kcc * TP:(kcc + 1) * TP]),
                                        reads=(("acc", kcc),), writes=(("xbf", kcc),))
                        entry = next_entry(kind)
                        assert entry["sel"] == (k, h)
                        wtok, so, skey = dma_chunk(entry, ps)
                        p.waitt("tensor", wtok)
                        for sc in range(SH):
                            if k == 0:
                                p.pre("tensor", writes=(("pb", sc),))
                            p.waitt("tensor", p.buf(("xbf", k)).w)
                            def fn(E, sc=sc, so=so, k=k, st_=(k == 0), sp=(k == KC - 1)):
                                return E.matmul(pb[sc][:, :],
                                                wring[:, so + sc * 128:so + sc * 128 + 128],
                                                xbf[:, k * TP:(k + 1) * TP],
                                                start=st_, stop=sp)
                            if sc == SH - 1:
                                tok = p.emit("tensor", fn, "TS")
                            else:
                                p.emit("tensor", fn, None)
                        p.post(tok, reads=[skey, ("xbf", k)],
                               writes=[("pb", sc2) for sc2 in range(SH)]
                               if k == KC - 1 else [])
                    for sc in range(SH):
                        scg = h * SH + sc
                        if sub == "a":
                            p.s(lambda E, sc=sc, scg=scg: E.copy(
                                aubf[:, scg * TP:(scg + 1) * TP], pb[sc][:, :]),
                                reads=(("pb", sc),), writes=(("aubf", scg),))
                        else:
                            p.s(lambda E, sc=sc, scg=scg: E.activation(
                                delta[:, scg * TP:(scg + 1) * TP], pb[sc][:, :],
                                AF.Identity, bias=lpn_s[:, scg:scg + 1]),
                                reads=(("pb", sc), "lpn"), writes=(("delta", scg),))

            def emit_tmp_chunk(m, sc):
                bi = (m - 1) % 2
                sl = slice(sc * TP, (sc + 1) * TP)
                sgi = sc % 2
                if m == 2:
                    p.v(lambda E, sl=sl: E.tensor_tensor(pw[:, sl], delta[:, sl],
                                                         delta[:, sl], ALU.mult),
                        reads=(("delta", sc),), writes=(("pw", sc),))
                elif m >= 3:
                    p.v(lambda E, sl=sl: E.tensor_tensor(pw[:, sl], pw[:, sl],
                                                         delta[:, sl], ALU.mult),
                        reads=(("pw", sc), ("delta", sc)), writes=(("pw", sc),))
                if m == 5:
                    p.v(lambda E, sl=sl: E.tensor_scalar(lnt[:, :], delta[:, sl],
                                                         THD, None, ALU.is_le),
                        reads=(("delta", sc),), writes=("lnt",))
                    p.v(lambda E, sl=sl: E.tensor_tensor(pw[:, sl], pw[:, sl],
                                                         lnt[:, :], ALU.mult),
                        reads=(("pw", sc), "lnt"), writes=(("pw", sc),))
                    p.v(lambda E, sl=sl: E.tensor_scalar(lnt[:, :], delta[:, sl],
                                                         -THD, None, ALU.is_ge),
                        reads=(("delta", sc),), writes=("lnt",))
                    p.v(lambda E, sl=sl: E.tensor_tensor(pw[:, sl], pw[:, sl],
                                                         lnt[:, :], ALU.mult),
                        reads=(("pw", sc), "lnt"), writes=(("pw", sc),))
                base, bkey = (delta, ("delta", sc)) if m == 1 else (pw, ("pw", sc))
                p.s(lambda E, sl=sl, m=m, base=base, sgi=sgi: E.activation(
                    sg[sgi][:, :], base[:, sl], AF.Identity,
                    scale=crep[:, m - 1:m]),
                    reads=(bkey, "crep"), writes=(("sg", sgi),))
                p.v(lambda E, sl=sl, bi=bi, sgi=sgi: E.tensor_tensor(
                    tmpb[bi][:, sl], sg[sgi][:, :], aubf[:, sl], ALU.mult),
                    reads=(("sg", sgi), ("aubf", sc)), writes=(("tmp", bi, sc),))

            for m in range(7):
                if m == 0:
                    movbuf, movkey = aubf, lambda kk: ("aubf", kk)
                else:
                    bi = (m - 1) % 2
                    movbuf, movkey = tmpb[bi], (lambda kk, bi=bi: ("tmp", bi, kk))
                for o in range(OC):
                    entry = next_entry("sterm")
                    assert entry["sel"] == (m, o)
                    dret = dma_chunk(entry, ps)
                    so = dret[1]
                    bank = (0, 1, 2, 3)[o % 4]
                    items = [((lambda so=so, kk=kk: wring[:, so + kk * 128:so + kk * 128 + 128]),
                              (lambda kk=kk, movbuf=movbuf: movbuf[:, kk * TP:(kk + 1) * TP]),
                              movkey(kk)) for kk in range(SC)]
                    tok = mm_group2(bank, [(dret, items)])
                    p.v(lambda E, o=o, bank=bank: E.tensor_tensor(
                        acc[:, o * TP:(o + 1) * TP], acc[:, o * TP:(o + 1) * TP],
                        pb[bank][:, :], ALU.add),
                        reads=(("pb", bank), ("acc", o)), writes=(("acc", o),))
                    if m < 6 and o % 2 == 0 and o // 2 < SC:
                        emit_tmp_chunk(m + 1, o // 2)

            fi = 0
            for si, sec in enumerate(c["SECS"]):
                for li, f in enumerate(sec):
                    gb = (2, 4, 6)[fi % 3]
                    ub = (3, 5, 7)[fi % 3]
                    for gu, bank in (("g", gb), ("u", ub)):
                        chunks = []
                        for q in range(4):
                            e_ = next_entry("fgu")
                            assert e_["sel"] == (gu, f, q)
                            dret = dma_chunk(e_, ps)
                            so = dret[1]
                            items = [((lambda so=so, kk=kk: wring[:, so + kk * 128:so + kk * 128 + 128]),
                                      (lambda kk=kk, q=q: xbf[:, (q * FQ + kk) * TP:(q * FQ + kk + 1) * TP]),
                                      ("xbf", q * FQ + kk)) for kk in range(FQ)]
                            chunks.append((dret, items))
                        mm_group2(bank, chunks)
                    sgi = fi % 2
                    p.s(lambda E, gb=gb, sgi=sgi: E.activation(
                        sg[sgi][:, :], pb[gb][:, :], AF.Silu),
                        reads=(("pb", gb),), writes=(("sg", sgi),))
                    p.v(lambda E, li=li, ub=ub, sgi=sgi: E.tensor_tensor(
                        vsec[:, li * TP:(li + 1) * TP], sg[sgi][:, :],
                        pb[ub][:, :], ALU.mult),
                        reads=(("sg", sgi), ("pb", ub)), writes=(("vsec", li),))
                    fi += 1
                runs = []
                i0 = 0
                while i0 < len(sec):
                    n = min(8, len(sec) - i0)
                    runs.append((i0, n))
                    i0 += n
                for o in range(OC):
                    chunks = []
                    for ri, (r0, rn) in enumerate(runs):
                        entry = next_entry("down")
                        assert entry["sel"] == (si, o, ri, r0, rn)
                        dret = dma_chunk(entry, ps)
                        so = dret[1]
                        items = [((lambda so=so, kk=kk: wring[:, so + kk * 128:so + kk * 128 + 128]),
                                  (lambda kk=kk, r0=r0: vsec[:, (r0 + kk) * TP:(r0 + kk + 1) * TP]),
                                  ("vsec", r0 + kk)) for kk in range(rn)]
                        chunks.append((dret, items))
                    bank = (0, 1, 6, 7)[o % 4]
                    tok = mm_group2(bank, chunks)
                    p.v(lambda E, o=o, bank=bank: E.tensor_tensor(
                        acc[:, o * TP:(o + 1) * TP], acc[:, o * TP:(o + 1) * TP],
                        pb[bank][:, :], ALU.add),
                        reads=(("pb", bank), ("acc", o)), writes=(("acc", o),))

            def layernorm(write_xbf):
                for cc in range(OC):
                    j = cc % 2
                    p.v(lambda E, cc=cc, j=j: E.tensor_copy(
                        statb[j][:, :], acc[:, cc * TP:(cc + 1) * TP]),
                        reads=(("acc", cc),), writes=(("statb", j),))
                    p.s(lambda E, cc=cc, j=j: E.activation(
                        sqb[j][:, :], acc[:, cc * TP:(cc + 1) * TP], AF.Square),
                        reads=(("acc", cc),), writes=(("sqb", j),))
                    p.pre("tensor", reads=(("statb", j), "onesk"),
                          writes=(("pb", 4),) if cc == 0 else ())
                    if cc == 0:
                        p.pre("tensor", writes=(("pb", 5),))
                    def f1(E, j=j, st_=(cc == 0), sp=(cc == OC - 1)):
                        return E.matmul(pb[4][0:1, :], onesk_s[:, :],
                                        statb[j][:, :], start=st_, stop=sp)
                    p.emit("tensor", f1, None)
                    p.pre("tensor", reads=(("sqb", j),))
                    def f2(E, j=j, st_=(cc == 0), sp=(cc == OC - 1)):
                        return E.matmul(pb[5][0:1, :], onesk_s[:, :],
                                        sqb[j][:, :], start=st_, stop=sp)
                    tok = p.emit("tensor", f2, "TS")
                    p.post(tok, reads=[("statb", j), ("sqb", j)],
                           writes=[("pb", 4), ("pb", 5)] if cc == OC - 1 else [])
                r0 = rows[0:1, 0:TP]
                r1 = rows[0:1, TP:2 * TP]
                r2 = rows[0:1, 2 * TP:3 * TP]
                p.v(lambda E: E.tensor_scalar(r0, pb[4][0:1, :], 1.0 / D, None, ALU.mult),
                    reads=(("pb", 4),), writes=("r0",))
                p.v(lambda E: E.tensor_scalar(r1, pb[5][0:1, :], 1.0 / D, None, ALU.mult),
                    reads=(("pb", 5),), writes=("r1",))
                p.v(lambda E: E.tensor_tensor(r2, r0, r0, ALU.mult),
                    reads=("r0",), writes=("r2",))
                p.v(lambda E: E.tensor_tensor(r1, r1, r2, ALU.subtract),
                    reads=("r1", "r2"), writes=("r1",))
                p.v(lambda E: E.tensor_scalar(r1, r1, EPS, None, ALU.add),
                    reads=("r1",), writes=("r1",))
                p.v(lambda E: E.reciprocal(r1, r1), reads=("r1",), writes=("r1",))
                p.s(lambda E: E.activation(r1, r1, AF.Sqrt), reads=("r1",), writes=("r1",))
                p.v(lambda E: E.tensor_tensor(r2, r0, r1, ALU.mult),
                    reads=("r0", "r1"), writes=("r2",))
                p.pre("tensor", reads=("r1", "onescol"), writes=(("pb", 6),))
                p.emit("tensor", lambda E: E.matmul(pb[6][:, :], onescol_s[:, :], r1,
                                                    start=True, stop=True), "TS")
                p.pre("tensor", reads=("r2",), writes=(("pb", 7),))
                tok = p.emit("tensor", lambda E: E.matmul(pb[7][:, :], onescol_s[:, :],
                                                          r2, start=True, stop=True), "TS")
                p.post(tok, reads=["r1", "r2"], writes=[("pb", 6), ("pb", 7)])
                for cc in range(OC):
                    p.v(lambda E, cc=cc: E.tensor_tensor(
                        lnt[:, :], acc[:, cc * TP:(cc + 1) * TP], pb[6][:, :], ALU.mult),
                        reads=(("acc", cc), ("pb", 6)), writes=("lnt",))
                    p.v(lambda E, cc=cc: E.tensor_tensor(
                        acc[:, cc * TP:(cc + 1) * TP], lnt[:, :], pb[7][:, :],
                        ALU.subtract),
                        reads=("lnt", ("pb", 7)), writes=(("acc", cc),))
                    if write_xbf:
                        p.s(lambda E, cc=cc: E.copy(xbf[:, cc * TP:(cc + 1) * TP],
                                                    acc[:, cc * TP:(cc + 1) * TP]),
                            reads=(("acc", cc),), writes=(("xbf", cc),))

            layernorm(write_xbf=True)

            er = next_entry("router")
            wtr, sor, skr = dma_chunk(er, ps)
            p.waitt("tensor", wtr)
            p.pre("tensor", writes=(("pb", 4),))
            for kt in range(KC):
                p.waitt("tensor", p.buf(("xbf", kt)).w)
                def fl(E, kt=kt, sor=sor, st_=(kt == 0), sp=(kt == KC - 1)):
                    return E.matmul(pb[4][0:NE, :], wring[:, sor + kt * 8:sor + kt * 8 + 8],
                                    xbf[:, kt * TP:(kt + 1) * TP], start=st_, stop=sp)
                if kt == KC - 1:
                    tok = p.emit("tensor", fl, "TS")
                else:
                    p.emit("tensor", fl, None)
            p.post(tok, reads=[skr] + XBFK, writes=[("pb", 4)])
            hchunks = []
            for q in range(4):
                ea = next_entry("lA")
                assert ea["sel"] == q
                dret = dma_chunk(ea, ps)
                soa = dret[1]
                items = [((lambda soa=soa, kt=kt: wring[:, soa + kt * 128:soa + kt * 128 + 128]),
                          (lambda kt=kt, q=q: xbf[:, (q * FQ + kt) * TP:(q * FQ + kt + 1) * TP]),
                          ("xbf", q * FQ + kt)) for kt in range(FQ)]
                hchunks.append((dret, items))
            htok = mm_group2(5, hchunks)
            p.v(lambda E: E.tensor_copy(wsb[:, :], pb[4][0:NE, :]),
                reads=(("pb", 4),), writes=("wsb",))
            for j in range(NT):
                p.pre("tensor", reads=("wsb", "ident"),
                      writes=(("pb", 7),) if j == 0 else ())
                def ft(E, j=j):
                    return E.transpose(pb[7][:, j * NE:(j + 1) * NE],
                                       wsb[0:NE, j * 128:(j + 1) * 128],
                                       ident_s[0:NE, 0:NE])
                if j == NT - 1:
                    tok = p.emit("tensor", ft, "TS")
                else:
                    p.emit("tensor", ft, None)
            p.post(tok, reads=["wsb"], writes=[("pb", 7)])
            p.v(lambda E: E.tensor_copy(lt[:, :], pb[7][:, 0:NT * NE]),
                reads=(("pb", 7),), writes=("lt",))
            for j in range(NT):
                ltj = lt[:, j * NE:(j + 1) * NE]
                p.v(lambda E, ltj=ltj: E.reduce_max(t2c[:, 0:1], ltj, axis=AX.X),
                    reads=("lt",), writes=("t2c",))
                p.v(lambda E: E.tensor_scalar(t2c[:, 1:2], t2c[:, 0:1], -1.0, None,
                                              ALU.mult),
                    reads=("t2c",), writes=("t2c",))
                p.v(lambda E, ltj=ltj: E.tensor_scalar(t2a[:, :], ltj, t2c[:, 0:1],
                                                       None, ALU.is_equal),
                    reads=("lt", "t2c"), writes=("t2a",))
                p.v(lambda E, ltj=ltj: E.scalar_tensor_tensor(
                    t2b[:, :], t2a[:, :], -1e30, ltj, ALU.mult, ALU.add),
                    reads=("t2a", "lt"), writes=("t2b",))
                p.v(lambda E: E.reduce_max(t2c[:, 2:3], t2b[:, :], axis=AX.X),
                    reads=("t2b",), writes=("t2c",))
                p.v(lambda E: E.tensor_scalar(t2b[:, :], t2b[:, :], t2c[:, 2:3],
                                              None, ALU.is_equal),
                    reads=("t2b", "t2c"), writes=("t2b",))
                p.v(lambda E: E.tensor_tensor(t2a[:, :], t2a[:, :], t2b[:, :], ALU.add),
                    reads=("t2a", "t2b"), writes=("t2a",))
                p.s(lambda E, ltj=ltj: E.activation(t2b[:, :], ltj, AF.Exp,
                                                    bias=t2c[:, 1:2]),
                    reads=("lt", "t2c"), writes=("t2b",))
                p.v(lambda E: E.tensor_tensor(t2a[:, :], t2b[:, :], t2a[:, :], ALU.mult),
                    reads=("t2b", "t2a"), writes=("t2a",))
                p.v(lambda E: E.reduce_sum(t2c[:, 3:4], t2a[:, :], axis=AX.X),
                    reads=("t2a",), writes=("t2c",))
                p.v(lambda E: E.reciprocal(t2c[:, 3:4], t2c[:, 3:4]),
                    reads=("t2c",), writes=("t2c",))
                p.v(lambda E: E.tensor_scalar(t2c[:, 3:4], t2c[:, 3:4], 2.0, None,
                                              ALU.mult),
                    reads=("t2c",), writes=("t2c",))
                p.v(lambda E, ltj=ltj: E.tensor_scalar(ltj, t2a[:, :], t2c[:, 3:4],
                                                       None, ALU.mult),
                    reads=("t2a", "t2c"), writes=("lt",))
            for o in range(OC):
                chunks = []
                for q in range(4):
                    entry = next_entry("b2")
                    assert entry["sel"] == (o, q)
                    dret = dma_chunk(entry, ps)
                    so = dret[1]
                    items = [((lambda so=so, kt=kt: wring[:, so + kt * 128:so + kt * 128 + 128]),
                              (lambda kt=kt, q=q: xbf[:, (q * FQ + kt) * TP:(q * FQ + kt + 1) * TP]),
                              ("xbf", q * FQ + kt)) for kt in range(FQ)]
                    chunks.append((dret, items))
                bank = (0, 1, 2, 3)[o % 4]
                tok = mm_group2(bank, chunks)
                j = o % 2
                p.s(lambda E, bank=bank, o=o, j=j: E.activation(
                    sg[j][:, :], pb[bank][:, :], AF.Identity, bias=b_s[:, o:o + 1]),
                    reads=(("pb", bank), "b"), writes=(("sg", j),))
                p.v(lambda E, o=o, j=j: E.tensor_tensor(
                    acc[:, o * TP:(o + 1) * TP], acc[:, o * TP:(o + 1) * TP],
                    sg[j][:, :], ALU.add),
                    reads=(("sg", j), ("acc", o)), writes=(("acc", o),))

            for j in range(NT):
                p.pre("tensor", reads=("lt", "ident"),
                      writes=(("pb", 7),) if j == 0 else ())
                def ft2(E, j=j):
                    return E.transpose(pb[7][0:NE, j * 128:(j + 1) * 128],
                                       lt[:, j * NE:(j + 1) * NE], ident_s[:, :])
                if j == NT - 1:
                    tok = p.emit("tensor", ft2, "TS")
                else:
                    p.emit("tensor", ft2, None)
            p.post(tok, reads=["lt"], writes=[("pb", 7)])
            p.v(lambda E: E.tensor_copy(wsb[:, :], pb[7][0:NE, :]),
                reads=(("pb", 7),), writes=("wsb",))
            p.pre("tensor", reads=("wsb", "rrep"), writes=(("pb", 6),))
            tok = p.emit("tensor", lambda E: E.matmul(pb[6][:, :], rrep_s[:, :],
                                                      wsb[:, :], start=True, stop=True),
                         "TS")
            p.post(tok, reads=["wsb", "rrep"], writes=[("pb", 6)])
            p.s(lambda E: E.copy(wrepsb[:, :], pb[6][:, :]),
                reads=(("pb", 6),), writes=("wrepsb",))
            p.v(lambda E: E.tensor_tensor(hbf[:, :], pb[5][:, :], wrepsb[:, :], ALU.mult),
                reads=(("pb", 5), "wrepsb"), writes=("hbf",))

            QW = OC // 4
            for q in range(4):
                el = next_entry("lorab")
                assert el["sel"] == q
                ldret = dma_chunk(el, ps)
                sol = ldret[1]
                for oo in range(QW):
                    o = q * QW + oo
                    bank = (0, 1, 2, 3)[o % 4]
                    items = [((lambda sol=sol, oo=oo: wring[:, sol + oo * 128:sol + oo * 128 + 128]),
                              (lambda: hbf[:, :]), "hbf")]
                    tok = mm_group2(bank, [(ldret if oo == 0 else None, items)])
                    p.post(tok, reads=[ldret[2]], writes=[])
                    p.v(lambda E, o=o, bank=bank: E.tensor_tensor(
                        acc[:, o * TP:(o + 1) * TP], acc[:, o * TP:(o + 1) * TP],
                        pb[bank][:, :], ALU.add),
                        reads=(("pb", bank), ("acc", o)), writes=(("acc", o),))

            layernorm(write_xbf=False)
            for g0 in range(0, OC, 4):
                p.o("sync", lambda E, t0=t0, g0=g0: E.dma_start(
                    outT.rearrange("(kc p) t -> p kc t", p=128)[:, g0:g0 + 4, t0:t0 + TP],
                    acc.rearrange("p (kc t) -> p kc t", kc=OC)[:, g0:g0 + 4, :]),
                    "XQ", reads=[("acc", cc) for cc in range(g0, g0 + 4)], n=16)
            assert plan_i[0] == len(plan)

        p.wait("sync", "XQ", p.cnt["XQ"])

        def replay(eng):
            def body(E):
                for item in p.prog[eng]:
                    if item[0] == "w":
                        E.wait_ge(SEM[item[1]], item[2])
                    else:
                        _, fn, inc, n = item
                        ins = fn(E)
                        if inc is not None:
                            ins.then_inc(SEM[inc], n)
            return body

        block.tensor(replay("tensor"))
        block.vector(replay("vector"))
        block.scalar(replay("scalar"))
        block.sync(replay("sync"))
        block.gpsimd(replay("gpsimd"))

    return nc, c, plan, total_cols


def stage_weights(c, plan, total_cols, inp):
    W = {
        "upT": np.ascontiguousarray(inp["L0_attn_up_w"].T),
        "gateT": np.ascontiguousarray(inp["L0_attn_gate_w"].T),
        "laoT": np.ascontiguousarray(inp["L0_attn_local_approx_output"].T),
        "fuseT": [np.ascontiguousarray(inp["L0_attn_fuse_weight"][:, :, m].T)
                  for m in range(c["G"])],
        "fgT": np.ascontiguousarray(inp["L0_attn_ffn_gate_w"].T),
        "fuT": np.ascontiguousarray(inp["L0_attn_ffn_up_w"].T),
        "fdT": np.ascontiguousarray(inp["L0_attn_ffn_down_w"].T),
        "wbT": np.ascontiguousarray(inp["L0_ffn_W_base"].T),
        "wrT": np.ascontiguousarray(inp["L0_ffn_W_router"].T),
        "lAT": np.ascontiguousarray(
            inp["L0_ffn_lora_A"].transpose(2, 0, 1).reshape(c["D"], c["E"] * c["R"])),
        "lBT": np.ascontiguousarray(
            inp["L0_ffn_lora_B"].transpose(0, 2, 1).reshape(c["E"] * c["R"], c["D"])),
    }
    stream = build_stream(c, plan, total_cols, W)
    smalls = {
        "lp_pack": np.ascontiguousarray(
            inp["L0_attn_local_point"].reshape(c["SC"], 128).T).astype(np.float32),
        "b_pack": np.ascontiguousarray(
            inp["L0_ffn_b_base"].reshape(c["OC"], 128).T).astype(np.float32),
        "disc_rep": np.ascontiguousarray(
            np.tile(inp["L0_attn_discount_factor"].reshape(1, c["G"]),
                    (128, 1))).astype(np.float32),
        "rrep": None,
        "ident": np.eye(128, dtype=np.float32),
        "ones_col": np.ones((1, 128), np.float32),
    }
    rrep = np.zeros((c["E"], 128), np.float32)
    for e in range(c["E"]):
        rrep[e, e * c["R"]:(e + 1) * c["R"]] = 1.0
    smalls["rrep"] = rrep
    return stream, smalls


def kernel(**inputs):
    inputs = {k: np.asarray(v, dtype=np.float32) for k, v in inputs.items()}
    cfg = real_cfg()
    c = derive(cfg)
    nc, c, plan, total_cols = build_core_program(cfg)
    stream, smalls = stage_weights(c, plan, total_cols, inputs)
    x = inputs["L0_attn_hidden_states"].reshape(c["TOKENS"], c["D"])
    in_maps = []
    for core in range(c["NCORES"]):
        xc = np.ascontiguousarray(
            x[core * c["TPC"]:(core + 1) * c["TPC"], :].T)
        m = {"xT": xc, "wstream": stream}
        m.update(smalls)
        in_maps.append(m)
    res = run_bass_kernel_spmd(nc, in_maps, core_ids=list(range(c["NCORES"])))
    outs = [r["outT"] for r in res.results]
    out = np.concatenate([o.T for o in outs], axis=0)
    return np.ascontiguousarray(
        out.reshape(2, c["TOKENS"] // 2, c["D"])).astype(np.float32)


# version 14
# speedup vs baseline: 1.0192x; 1.0192x over previous
import contextlib
import numpy as np
import concourse.bass as bass
import concourse.mybir as mybir
from concourse.bass_utils import run_bass_kernel_spmd

F32 = mybir.dt.float32
BF16 = mybir.dt.bfloat16
AF = mybir.ActivationFunctionType
ALU = mybir.AluOpType
AX = mybir.AxisListType

SEMNAMES = ("CQ", "XQ", "XB", "XC", "XD", "XE", "TS", "VS", "SS", "W0", "W1", "W2", "W3", "W4", "W5", "W6")
ENGS = ("tensor", "vector", "scalar", "sync", "gpsimd")


def real_cfg():
    return dict(D=4096, S=1024, FF=9984, E=8, R=16, TOKENS=8192, NCORES=8, TP=512,
                THD=2.5, EPS=1e-5, G=6)


def derive(cfg):
    c = dict(cfg)
    c["KC"] = c["D"] // 128
    c["SC"] = c["S"] // 128
    c["FC"] = c["FF"] // 128
    c["OC"] = c["D"] // 128
    assert c["E"] * c["R"] == 128
    secs, i = [], 0
    while i < c["FC"]:
        n = min(20, c["FC"] - i)
        secs.append(list(range(i, i + n)))
        i += n
    c["SECS"] = secs
    c["VSEC"] = max(len(s) for s in secs)
    c["NPASS"] = c["TOKENS"] // c["NCORES"] // c["TP"]
    c["TPC"] = c["TOKENS"] // c["NCORES"]
    c["NT"] = c["TP"] // 128
    return c


def make_plan(c):
    plan = []

    def add(kind, sel, cols, carve):
        plan.append({"kind": kind, "sel": sel, "cols": cols, "carve": carve})

    SH = c["SC"] // 2
    for h in range(2):
        for k in range(c["KC"]):
            add("upT_row", (k, h), SH * 128, "A")
    for h in range(2):
        for k in range(c["KC"]):
            add("gateT_row", (k, h), SH * 128, "A")
    for m in range(7):
        for o in range(c["OC"]):
            add("sterm", (m, o), c["SC"] * 128, "A")
    FH = c["KC"] // 2
    FQ = c["KC"] // 4
    for si, sec in enumerate(c["SECS"]):
        for f in sec:
            for gu in ("g", "u"):
                for q in range(4):
                    add("fgu", (gu, f, q), FQ * 128, "A")
        runs = []
        i0 = 0
        while i0 < len(sec):
            n = min(8, len(sec) - i0)
            runs.append((i0, n))
            i0 += n
        for o in range(c["OC"]):
            for ri, (r0, rn) in enumerate(runs):
                add("down", (si, o, ri, r0, rn), rn * 128, "A")
    add("router", None, c["KC"] * 8, "A")
    for q in range(4):
        add("lA", q, FQ * 128, "A")
    for o in range(c["OC"]):
        for q in range(4):
            add("b2", (o, q), FQ * 128, "A")
    for q in range(4):
        add("lorab", q, (c["OC"] // 4) * 128, "A")
    off = 0
    for p in plan:
        p["off"] = off
        off += p["cols"]
    return plan, off


def _img_tiles(block):
    n = block.shape[0] // 128
    w = block.shape[1]
    return np.ascontiguousarray(
        block.reshape(n, 128, w).transpose(1, 0, 2).reshape(128, n * w))


def build_stream(c, plan, total_cols, W):
    out = np.empty((128, total_cols), np.float32)
    for p in plan:
        k, sel, off, cols = p["kind"], p["sel"], p["off"], p["cols"]
        if k in ("upT_row", "gateT_row"):
            kk, h = sel
            srcw = W["upT"] if k == "upT_row" else W["gateT"]
            SH = srcw.shape[1] // 256
            img = srcw[kk * 128:(kk + 1) * 128, h * SH * 128:(h + 1) * SH * 128]
        elif k == "sterm":
            m, o = sel
            src = W["laoT"] if m == 0 else W["fuseT"][m - 1]
            img = _img_tiles(src[:, o * 128:(o + 1) * 128])
        elif k == "fgu":
            gu, f, q = sel
            srcw = W["fgT"] if gu == "g" else W["fuT"]
            FQ = srcw.shape[0] // 512
            img = _img_tiles(srcw[q * FQ * 128:(q + 1) * FQ * 128,
                                  f * 128:(f + 1) * 128])
        elif k == "down":
            si, o, ri, r0, rn = sel
            sec = c["SECS"][si]
            f0 = sec[0] + r0
            img = _img_tiles(W["fdT"][f0 * 128:(f0 + rn) * 128,
                                      o * 128:(o + 1) * 128])
        elif k == "router":
            img = _img_tiles(W["wrT"])
        elif k == "lA":
            q = sel
            FQ = W["lAT"].shape[0] // 512
            img = _img_tiles(W["lAT"][q * FQ * 128:(q + 1) * FQ * 128, :])
        elif k == "b2":
            o, q = sel
            FQ = W["wbT"].shape[0] // 512
            img = _img_tiles(W["wbT"][q * FQ * 128:(q + 1) * FQ * 128,
                                      o * 128:(o + 1) * 128])
        elif k == "lorab":
            q = sel
            qw = W["lBT"].shape[1] // 4
            img = W["lBT"][:, q * qw:(q + 1) * qw]
        else:
            raise KeyError(k)
        assert img.shape == (128, cols), (k, img.shape, cols)
        out[:, off:off + cols] = img
    return out


class Buf:
    __slots__ = ("w", "rs")

    def __init__(self):
        self.w = None
        self.rs = []


class P:
    def __init__(self):
        self.prog = {e: [] for e in ENGS}
        self.cnt = dict.fromkeys(SEMNAMES, 0)
        self.wm = {(e, s): 0 for e in ENGS for s in SEMNAMES}
        self.R = {}

    def wait(self, eng, sem, val):
        if val is None or val <= 0 or self.wm[(eng, sem)] >= val:
            return
        self.wm[(eng, sem)] = val
        self.prog[eng].append(("w", sem, val))

    def waitt(self, eng, tok):
        if tok is not None:
            self.wait(eng, tok[0], tok[1])

    def buf(self, key):
        b = self.R.get(key)
        if b is None:
            b = self.R[key] = Buf()
        return b

    def pre(self, eng, reads=(), writes=()):
        need = {}
        def want(t):
            if t is not None and t[1] > need.get(t[0], 0):
                need[t[0]] = t[1]
        for key in reads:
            want(self.buf(key).w)
        for key in writes:
            b = self.buf(key)
            want(b.w)
            for t in b.rs:
                want(t)
        for sem, val in need.items():
            self.wait(eng, sem, val)

    def post(self, tok, reads=(), writes=(), const_reads=()):
        for key in writes:
            b = self.buf(key)
            b.w = tok
            b.rs = []
        for key in reads:
            self.buf(key).rs.append(tok)

    def emit(self, eng, fn, inc=None, n=1):
        tok = None
        if inc is not None:
            self.cnt[inc] += n
            tok = (inc, self.cnt[inc])
        self.prog[eng].append(("o", fn, inc, n))
        return tok

    def o(self, eng, fn, inc, reads=(), writes=(), n=1):
        self.pre(eng, reads, writes)
        tok = self.emit(eng, fn, inc, n)
        self.post(tok, reads, writes)
        return tok

    def v(self, fn, reads=(), writes=()):
        return self.o("vector", fn, "VS", reads, writes)

    def s(self, fn, reads=(), writes=()):
        return self.o("scalar", fn, "SS", reads, writes)


def build_core_program(cfg):
    c = derive(cfg)
    D, S, FF, NE, R = c["D"], c["S"], c["FF"], c["E"], c["R"]
    KC, SC, FC, OC, TP = c["KC"], c["SC"], c["FC"], c["OC"], c["TP"]
    NPASS, TPC, NT = c["NPASS"], c["TPC"], c["NT"]
    THD, EPS = c["THD"], c["EPS"]
    plan, total_cols = make_plan(c)

    FH = KC // 2
    FQ = KC // 4
    CARVE = {"A": (max(e["cols"] for e in plan), 6)}
    ARENA = max(sc * ns for sc, ns in CARVE.values())

    nc = bass.Bass("TRN2", target_bir_lowering=False)
    xT = nc.dram_tensor("xT", [D, TPC], F32, kind="ExternalInput").ap()
    ws = nc.dram_tensor("wstream", [128, total_cols], F32, kind="ExternalInput").ap()
    lp_h = nc.dram_tensor("lp_pack", [128, SC], F32, kind="ExternalInput").ap()
    b_h = nc.dram_tensor("b_pack", [128, OC], F32, kind="ExternalInput").ap()
    disc_h = nc.dram_tensor("disc_rep", [128, 6], F32, kind="ExternalInput").ap()
    rrep_h = nc.dram_tensor("rrep", [NE, 128], F32, kind="ExternalInput").ap()
    ident_h = nc.dram_tensor("ident", [128, 128], F32, kind="ExternalInput").ap()
    onescol_h = nc.dram_tensor("ones_col", [1, 128], F32, kind="ExternalInput").ap()
    outT = nc.dram_tensor("outT", [D, TPC], F32, kind="ExternalOutput").ap()

    st = contextlib.ExitStack()
    sb = lambda nm, shp, dt: st.enter_context(nc.sbuf_tensor(nm, shp, dt))
    with st:
        acc = sb("acc", [128, OC * TP], F32)
        xbf = sb("xbf", [128, KC * TP], BF16)
        aubf = sb("aubf", [128, SC * TP], BF16)
        delta = sb("delta", [128, SC * TP], F32)
        pw = sb("pw", [128, SC * TP], F32)
        tmpb = [sb("tmp0", [128, SC * TP], BF16), sb("tmp1", [128, SC * TP], BF16)]
        vsec = sb("vsec", [128, c["VSEC"] * TP], BF16)
        wring = sb("wring", [128, ARENA], BF16)
        lp_s = sb("lp_s", [128, SC], F32)
        lpn_s = sb("lpn_s", [128, SC], F32)
        b_s = sb("b_s", [128, OC], F32)
        crep = sb("crep", [128, 6], F32)
        rrep_s = sb("rrep_s", [NE, 128], F32)
        ident_s = sb("ident_s", [128, 128], F32)
        onescol_s = sb("ones_col_s", [1, 128], F32)
        onesk_s = sb("onesk_s", [128, 1], BF16)
        sg = [sb("sgA", [128, TP], F32), sb("sgB", [128, TP], F32)]
        statb = [sb("statA", [128, TP], BF16), sb("statB", [128, TP], BF16)]
        sqb = [sb("sqA", [128, TP], BF16), sb("sqB", [128, TP], BF16)]
        rows = sb("rows", [1, 3 * TP], F32)
        lnt = sb("lnt", [128, TP], F32)
        lt = sb("lt", [128, NT * NE], F32)
        t2a = sb("t2a", [128, NE], F32)
        t2b = sb("t2b", [128, NE], F32)
        t2c = sb("t2c", [128, 4], F32)
        wsb = sb("wsb", [NE, TP], F32)
        wrepsb = sb("wrepsb", [128, TP], F32)
        hbf = sb("hbf", [128, TP], BF16)
        pb = [st.enter_context(nc.psum_tensor("pb%d" % i, [128, TP], F32))
              for i in range(8)]
        SEM = {nm: st.enter_context(nc.semaphore(nm)) for nm in SEMNAMES}
        block = st.enter_context(nc.Block())

        p = P()
        ACCK = [("acc", o) for o in range(OC)]
        XBFK = [("xbf", k) for k in range(KC)]

        plan_by_carve_pos = {}
        cur_carve = [None]
        carve_keys = [set()]

        def dma_chunk(entry, pass_i):
            carve = entry["carve"]
            cols, nslot = CARVE[carve]
            if cur_carve[0] != carve:
                for key in carve_keys[0]:
                    b = p.buf(key)
                    p.waitt("gpsimd", b.w)
                    for t in b.rs:
                        p.waitt("gpsimd", t)
                    del p.R[key]
                cur_carve[0] = carve
                carve_keys[0] = set()
                plan_by_carve_pos[carve] = 0
            i = plan_by_carve_pos[carve]
            plan_by_carve_pos[carve] = i + 1
            slot = i % nslot
            so = slot * cols
            key = ("slot", carve, slot)
            carve_keys[0].add(key)
            ccols = entry["cols"]
            off = entry["off"]
            sem = "W%d" % slot
            tok = p.o("gpsimd",
                      lambda E, off=off, ccols=ccols, so=so:
                      E.dma_start(wring[:, so:so + ccols], ws[:, off:off + ccols]),
                      sem, writes=(key,), n=16)
            return tok, so, key

        def mm_group2(bank, chunks, extra_reads=()):
            p.pre("tensor", reads=(), writes=(("pb", bank),))
            tot = sum(len(it) for _, it in chunks)
            i = 0
            tok = None
            for dret, items in chunks:
                need = {}
                if dret is not None:
                    need[dret[0][0]] = max(need.get(dret[0][0], 0), dret[0][1])
                mov_keys = set()
                for sf, mf, mk in items:
                    if mk is not None:
                        mov_keys.add(mk)
                        t = p.buf(mk).w
                        if t is not None:
                            need[t[0]] = max(need.get(t[0], 0), t[1])
                for sem, val in need.items():
                    p.wait("tensor", sem, val)
                n_it = len(items)
                for j, (sf, mf, mk) in enumerate(items):
                    def fn(E, b=bank, sf=sf, mf=mf, st_=(i == 0), sp=(i == tot - 1)):
                        return E.matmul(pb[b][:, :], sf(), mf(), start=st_, stop=sp)
                    if j == n_it - 1:
                        tok = p.emit("tensor", fn, "TS")
                    else:
                        p.emit("tensor", fn, None)
                    i += 1
                rd = list(mov_keys)
                if dret is not None:
                    rd.append(dret[2])
                p.post(tok, reads=rd, writes=[])
            p.post(tok, reads=list(extra_reads), writes=[("pb", bank)])
            return tok

        const_keys = []
        for ap_s, ap_h, key in ((lp_s, lp_h, "lp"), (b_s, b_h, "b"),
                                (crep, disc_h, "crep"), (rrep_s, rrep_h, "rrep"),
                                (ident_s, ident_h, "ident"),
                                (onescol_s, onescol_h, "onescol")):
            p.emit("sync", lambda E, d=ap_s, s_=ap_h: E.dma_start(d[:, :], s_[:, :]),
                   "CQ", 16)
            const_keys.append(key)
        ctok = ("CQ", p.cnt["CQ"])
        for key in const_keys:
            p.buf(key).w = ctok
        p.s(lambda E: E.activation(crep[:, :], crep[:, :], AF.Exp, scale=-1.0),
            reads=("crep",), writes=("crep",))
        p.v(lambda E: E.tensor_scalar(lpn_s[:, :], lp_s[:, :], -1.0, None, ALU.mult),
            reads=("lp",), writes=("lpn",))
        p.v(lambda E: E.tensor_scalar(onesk_s[:, :], b_s[:, 0:1], 0.0, 1.0,
                                      ALU.mult, ALU.add),
            reads=("b",), writes=("onesk",))

        for ps in range(NPASS):
            t0 = ps * TP
            p.o("sync", lambda E, t0=t0: E.dma_start(
                acc.rearrange("p (kc t) -> p kc t", kc=OC),
                xT.rearrange("(kc p) t -> p kc t", p=128)[:, :, t0:t0 + TP]),
                "XQ", writes=ACCK, n=16)
            QK = KC // 4

            def emit_xbf_quarter(qi, t0=t0, QK=QK):
                qsem = ("XB", "XC", "XD", "XE")[qi]
                p.o("gpsimd", lambda E, t0=t0, QK=QK, qi=qi: E.dma_start(
                    xbf.rearrange("p (kc t) -> p kc t", kc=KC)[:, qi * QK:(qi + 1) * QK, :],
                    xT.rearrange("(kc p) t -> p kc t", p=128)[:, qi * QK:(qi + 1) * QK, t0:t0 + TP]),
                    qsem, writes=XBFK[qi * QK:(qi + 1) * QK], n=16)

            plan_i = [0]

            def next_entry(expect_kind=None):
                e = plan[plan_i[0]]
                plan_i[0] += 1
                if expect_kind is not None:
                    assert e["kind"] == expect_kind, (e, expect_kind)
                return e

            SH = SC // 2
            for sub, kind in (("a", "upT_row"), ("b", "gateT_row")):
                for h in range(2):
                    for k in range(KC):
                        if sub == "a" and h == 0 and k % QK == 0 and (
                                ps > 0 or k // QK < 2):
                            emit_xbf_quarter(k // QK)
                        if ps == 0 and sub == "a" and h == 0 and k >= 2 * QK \
                                and (k - 2 * QK) % 2 == 0:
                            for kcc in (k, k + 1):
                                if kcc < KC:
                                    p.s(lambda E, kcc=kcc: E.copy(
                                        xbf[:, kcc * TP:(kcc + 1) * TP],
                                        acc[:, kcc * TP:(kcc + 1) * TP]),
                                        reads=(("acc", kcc),), writes=(("xbf", kcc),))
                        entry = next_entry(kind)
                        assert entry["sel"] == (k, h)
                        wtok, so, skey = dma_chunk(entry, ps)
                        p.waitt("tensor", wtok)
                        for sc in range(SH):
                            if k == 0:
                                p.pre("tensor", writes=(("pb", sc),))
                            p.waitt("tensor", p.buf(("xbf", k)).w)
                            def fn(E, sc=sc, so=so, k=k, st_=(k == 0), sp=(k == KC - 1)):
                                return E.matmul(pb[sc][:, :],
                                                wring[:, so + sc * 128:so + sc * 128 + 128],
                                                xbf[:, k * TP:(k + 1) * TP],
                                                start=st_, stop=sp)
                            if sc == SH - 1:
                                tok = p.emit("tensor", fn, "TS")
                            else:
                                p.emit("tensor", fn, None)
                        p.post(tok, reads=[skey, ("xbf", k)],
                               writes=[("pb", sc2) for sc2 in range(SH)]
                               if k == KC - 1 else [])
                    for sc in range(SH):
                        scg = h * SH + sc
                        if sub == "a":
                            p.s(lambda E, sc=sc, scg=scg: E.copy(
                                aubf[:, scg * TP:(scg + 1) * TP], pb[sc][:, :]),
                                reads=(("pb", sc),), writes=(("aubf", scg),))
                        else:
                            p.s(lambda E, sc=sc, scg=scg: E.activation(
                                delta[:, scg * TP:(scg + 1) * TP], pb[sc][:, :],
                                AF.Identity, bias=lpn_s[:, scg:scg + 1]),
                                reads=(("pb", sc), "lpn"), writes=(("delta", scg),))

            def emit_tmp_chunk(m, sc):
                bi = (m - 1) % 2
                sl = slice(sc * TP, (sc + 1) * TP)
                sgi = sc % 2
                if m == 2:
                    p.v(lambda E, sl=sl: E.tensor_tensor(pw[:, sl], delta[:, sl],
                                                         delta[:, sl], ALU.mult),
                        reads=(("delta", sc),), writes=(("pw", sc),))
                elif m >= 3:
                    p.v(lambda E, sl=sl: E.tensor_tensor(pw[:, sl], pw[:, sl],
                                                         delta[:, sl], ALU.mult),
                        reads=(("pw", sc), ("delta", sc)), writes=(("pw", sc),))
                if m == 5:
                    p.v(lambda E, sl=sl: E.tensor_scalar(lnt[:, :], delta[:, sl],
                                                         THD, None, ALU.is_le),
                        reads=(("delta", sc),), writes=("lnt",))
                    p.v(lambda E, sl=sl: E.tensor_tensor(pw[:, sl], pw[:, sl],
                                                         lnt[:, :], ALU.mult),
                        reads=(("pw", sc), "lnt"), writes=(("pw", sc),))
                    p.v(lambda E, sl=sl: E.tensor_scalar(lnt[:, :], delta[:, sl],
                                                         -THD, None, ALU.is_ge),
                        reads=(("delta", sc),), writes=("lnt",))
                    p.v(lambda E, sl=sl: E.tensor_tensor(pw[:, sl], pw[:, sl],
                                                         lnt[:, :], ALU.mult),
                        reads=(("pw", sc), "lnt"), writes=(("pw", sc),))
                base, bkey = (delta, ("delta", sc)) if m == 1 else (pw, ("pw", sc))
                p.s(lambda E, sl=sl, m=m, base=base, sgi=sgi: E.activation(
                    sg[sgi][:, :], base[:, sl], AF.Identity,
                    scale=crep[:, m - 1:m]),
                    reads=(bkey, "crep"), writes=(("sg", sgi),))
                p.v(lambda E, sl=sl, bi=bi, sgi=sgi: E.tensor_tensor(
                    tmpb[bi][:, sl], sg[sgi][:, :], aubf[:, sl], ALU.mult),
                    reads=(("sg", sgi), ("aubf", sc)), writes=(("tmp", bi, sc),))

            for m in range(7):
                if m == 0:
                    movbuf, movkey = aubf, lambda kk: ("aubf", kk)
                else:
                    bi = (m - 1) % 2
                    movbuf, movkey = tmpb[bi], (lambda kk, bi=bi: ("tmp", bi, kk))
                for o in range(OC):
                    entry = next_entry("sterm")
                    assert entry["sel"] == (m, o)
                    dret = dma_chunk(entry, ps)
                    so = dret[1]
                    bank = (0, 1, 2, 3)[o % 4]
                    items = [((lambda so=so, kk=kk: wring[:, so + kk * 128:so + kk * 128 + 128]),
                              (lambda kk=kk, movbuf=movbuf: movbuf[:, kk * TP:(kk + 1) * TP]),
                              movkey(kk)) for kk in range(SC)]
                    tok = mm_group2(bank, [(dret, items)])
                    p.v(lambda E, o=o, bank=bank: E.tensor_tensor(
                        acc[:, o * TP:(o + 1) * TP], acc[:, o * TP:(o + 1) * TP],
                        pb[bank][:, :], ALU.add),
                        reads=(("pb", bank), ("acc", o)), writes=(("acc", o),))
                    if m < 6 and o % 2 == 0 and o // 2 < SC:
                        emit_tmp_chunk(m + 1, o // 2)

            fi = 0
            for si, sec in enumerate(c["SECS"]):
                for li, f in enumerate(sec):
                    gb = 2 + 2 * (fi % 2)
                    ub = 3 + 2 * (fi % 2)
                    for gu, bank in (("g", gb), ("u", ub)):
                        chunks = []
                        for q in range(4):
                            e_ = next_entry("fgu")
                            assert e_["sel"] == (gu, f, q)
                            dret = dma_chunk(e_, ps)
                            so = dret[1]
                            items = [((lambda so=so, kk=kk: wring[:, so + kk * 128:so + kk * 128 + 128]),
                                      (lambda kk=kk, q=q: xbf[:, (q * FQ + kk) * TP:(q * FQ + kk + 1) * TP]),
                                      ("xbf", q * FQ + kk)) for kk in range(FQ)]
                            chunks.append((dret, items))
                        mm_group2(bank, chunks)
                    sgi = fi % 2
                    p.s(lambda E, gb=gb, sgi=sgi: E.activation(
                        sg[sgi][:, :], pb[gb][:, :], AF.Silu),
                        reads=(("pb", gb),), writes=(("sg", sgi),))
                    p.v(lambda E, li=li, ub=ub, sgi=sgi: E.tensor_tensor(
                        vsec[:, li * TP:(li + 1) * TP], sg[sgi][:, :],
                        pb[ub][:, :], ALU.mult),
                        reads=(("sg", sgi), ("pb", ub)), writes=(("vsec", li),))
                    fi += 1
                runs = []
                i0 = 0
                while i0 < len(sec):
                    n = min(8, len(sec) - i0)
                    runs.append((i0, n))
                    i0 += n
                for o in range(OC):
                    chunks = []
                    for ri, (r0, rn) in enumerate(runs):
                        entry = next_entry("down")
                        assert entry["sel"] == (si, o, ri, r0, rn)
                        dret = dma_chunk(entry, ps)
                        so = dret[1]
                        items = [((lambda so=so, kk=kk: wring[:, so + kk * 128:so + kk * 128 + 128]),
                                  (lambda kk=kk, r0=r0: vsec[:, (r0 + kk) * TP:(r0 + kk + 1) * TP]),
                                  ("vsec", r0 + kk)) for kk in range(rn)]
                        chunks.append((dret, items))
                    bank = (0, 1, 6, 7)[o % 4]
                    tok = mm_group2(bank, chunks)
                    p.v(lambda E, o=o, bank=bank: E.tensor_tensor(
                        acc[:, o * TP:(o + 1) * TP], acc[:, o * TP:(o + 1) * TP],
                        pb[bank][:, :], ALU.add),
                        reads=(("pb", bank), ("acc", o)), writes=(("acc", o),))

            def layernorm(write_xbf):
                for cc in range(OC):
                    j = cc % 2
                    p.v(lambda E, cc=cc, j=j: E.tensor_copy(
                        statb[j][:, :], acc[:, cc * TP:(cc + 1) * TP]),
                        reads=(("acc", cc),), writes=(("statb", j),))
                    p.s(lambda E, cc=cc, j=j: E.activation(
                        sqb[j][:, :], acc[:, cc * TP:(cc + 1) * TP], AF.Square),
                        reads=(("acc", cc),), writes=(("sqb", j),))
                    p.pre("tensor", reads=(("statb", j), "onesk"),
                          writes=(("pb", 4),) if cc == 0 else ())
                    if cc == 0:
                        p.pre("tensor", writes=(("pb", 5),))
                    def f1(E, j=j, st_=(cc == 0), sp=(cc == OC - 1)):
                        return E.matmul(pb[4][0:1, :], onesk_s[:, :],
                                        statb[j][:, :], start=st_, stop=sp)
                    p.emit("tensor", f1, None)
                    p.pre("tensor", reads=(("sqb", j),))
                    def f2(E, j=j, st_=(cc == 0), sp=(cc == OC - 1)):
                        return E.matmul(pb[5][0:1, :], onesk_s[:, :],
                                        sqb[j][:, :], start=st_, stop=sp)
                    tok = p.emit("tensor", f2, "TS")
                    p.post(tok, reads=[("statb", j), ("sqb", j)],
                           writes=[("pb", 4), ("pb", 5)] if cc == OC - 1 else [])
                r0 = rows[0:1, 0:TP]
                r1 = rows[0:1, TP:2 * TP]
                r2 = rows[0:1, 2 * TP:3 * TP]
                p.v(lambda E: E.tensor_scalar(r0, pb[4][0:1, :], 1.0 / D, None, ALU.mult),
                    reads=(("pb", 4),), writes=("r0",))
                p.v(lambda E: E.tensor_scalar(r1, pb[5][0:1, :], 1.0 / D, None, ALU.mult),
                    reads=(("pb", 5),), writes=("r1",))
                p.v(lambda E: E.tensor_tensor(r2, r0, r0, ALU.mult),
                    reads=("r0",), writes=("r2",))
                p.v(lambda E: E.tensor_tensor(r1, r1, r2, ALU.subtract),
                    reads=("r1", "r2"), writes=("r1",))
                p.v(lambda E: E.tensor_scalar(r1, r1, EPS, None, ALU.add),
                    reads=("r1",), writes=("r1",))
                p.v(lambda E: E.reciprocal(r1, r1), reads=("r1",), writes=("r1",))
                p.s(lambda E: E.activation(r1, r1, AF.Sqrt), reads=("r1",), writes=("r1",))
                p.v(lambda E: E.tensor_tensor(r2, r0, r1, ALU.mult),
                    reads=("r0", "r1"), writes=("r2",))
                p.pre("tensor", reads=("r1", "onescol"), writes=(("pb", 6),))
                p.emit("tensor", lambda E: E.matmul(pb[6][:, :], onescol_s[:, :], r1,
                                                    start=True, stop=True), "TS")
                p.pre("tensor", reads=("r2",), writes=(("pb", 7),))
                tok = p.emit("tensor", lambda E: E.matmul(pb[7][:, :], onescol_s[:, :],
                                                          r2, start=True, stop=True), "TS")
                p.post(tok, reads=["r1", "r2"], writes=[("pb", 6), ("pb", 7)])
                for cc in range(OC):
                    p.v(lambda E, cc=cc: E.tensor_tensor(
                        lnt[:, :], acc[:, cc * TP:(cc + 1) * TP], pb[6][:, :], ALU.mult),
                        reads=(("acc", cc), ("pb", 6)), writes=("lnt",))
                    p.v(lambda E, cc=cc: E.tensor_tensor(
                        acc[:, cc * TP:(cc + 1) * TP], lnt[:, :], pb[7][:, :],
                        ALU.subtract),
                        reads=("lnt", ("pb", 7)), writes=(("acc", cc),))
                    if write_xbf:
                        p.s(lambda E, cc=cc: E.copy(xbf[:, cc * TP:(cc + 1) * TP],
                                                    acc[:, cc * TP:(cc + 1) * TP]),
                            reads=(("acc", cc),), writes=(("xbf", cc),))

            layernorm(write_xbf=True)

            er = next_entry("router")
            wtr, sor, skr = dma_chunk(er, ps)
            p.waitt("tensor", wtr)
            p.pre("tensor", writes=(("pb", 4),))
            for kt in range(KC):
                p.waitt("tensor", p.buf(("xbf", kt)).w)
                def fl(E, kt=kt, sor=sor, st_=(kt == 0), sp=(kt == KC - 1)):
                    return E.matmul(pb[4][0:NE, :], wring[:, sor + kt * 8:sor + kt * 8 + 8],
                                    xbf[:, kt * TP:(kt + 1) * TP], start=st_, stop=sp)
                if kt == KC - 1:
                    tok = p.emit("tensor", fl, "TS")
                else:
                    p.emit("tensor", fl, None)
            p.post(tok, reads=[skr] + XBFK, writes=[("pb", 4)])
            hchunks = []
            for q in range(4):
                ea = next_entry("lA")
                assert ea["sel"] == q
                dret = dma_chunk(ea, ps)
                soa = dret[1]
                items = [((lambda soa=soa, kt=kt: wring[:, soa + kt * 128:soa + kt * 128 + 128]),
                          (lambda kt=kt, q=q: xbf[:, (q * FQ + kt) * TP:(q * FQ + kt + 1) * TP]),
                          ("xbf", q * FQ + kt)) for kt in range(FQ)]
                hchunks.append((dret, items))
            htok = mm_group2(5, hchunks)
            p.v(lambda E: E.tensor_copy(wsb[:, :], pb[4][0:NE, :]),
                reads=(("pb", 4),), writes=("wsb",))
            for j in range(NT):
                p.pre("tensor", reads=("wsb", "ident"),
                      writes=(("pb", 7),) if j == 0 else ())
                def ft(E, j=j):
                    return E.transpose(pb[7][:, j * NE:(j + 1) * NE],
                                       wsb[0:NE, j * 128:(j + 1) * 128],
                                       ident_s[0:NE, 0:NE])
                if j == NT - 1:
                    tok = p.emit("tensor", ft, "TS")
                else:
                    p.emit("tensor", ft, None)
            p.post(tok, reads=["wsb"], writes=[("pb", 7)])
            p.v(lambda E: E.tensor_copy(lt[:, :], pb[7][:, 0:NT * NE]),
                reads=(("pb", 7),), writes=("lt",))
            for j in range(NT):
                ltj = lt[:, j * NE:(j + 1) * NE]
                p.v(lambda E, ltj=ltj: E.reduce_max(t2c[:, 0:1], ltj, axis=AX.X),
                    reads=("lt",), writes=("t2c",))
                p.v(lambda E: E.tensor_scalar(t2c[:, 1:2], t2c[:, 0:1], -1.0, None,
                                              ALU.mult),
                    reads=("t2c",), writes=("t2c",))
                p.v(lambda E, ltj=ltj: E.tensor_scalar(t2a[:, :], ltj, t2c[:, 0:1],
                                                       None, ALU.is_equal),
                    reads=("lt", "t2c"), writes=("t2a",))
                p.v(lambda E, ltj=ltj: E.scalar_tensor_tensor(
                    t2b[:, :], t2a[:, :], -1e30, ltj, ALU.mult, ALU.add),
                    reads=("t2a", "lt"), writes=("t2b",))
                p.v(lambda E: E.reduce_max(t2c[:, 2:3], t2b[:, :], axis=AX.X),
                    reads=("t2b",), writes=("t2c",))
                p.v(lambda E: E.tensor_scalar(t2b[:, :], t2b[:, :], t2c[:, 2:3],
                                              None, ALU.is_equal),
                    reads=("t2b", "t2c"), writes=("t2b",))
                p.v(lambda E: E.tensor_tensor(t2a[:, :], t2a[:, :], t2b[:, :], ALU.add),
                    reads=("t2a", "t2b"), writes=("t2a",))
                p.s(lambda E, ltj=ltj: E.activation(t2b[:, :], ltj, AF.Exp,
                                                    bias=t2c[:, 1:2]),
                    reads=("lt", "t2c"), writes=("t2b",))
                p.v(lambda E: E.tensor_tensor(t2a[:, :], t2b[:, :], t2a[:, :], ALU.mult),
                    reads=("t2b", "t2a"), writes=("t2a",))
                p.v(lambda E: E.reduce_sum(t2c[:, 3:4], t2a[:, :], axis=AX.X),
                    reads=("t2a",), writes=("t2c",))
                p.v(lambda E: E.reciprocal(t2c[:, 3:4], t2c[:, 3:4]),
                    reads=("t2c",), writes=("t2c",))
                p.v(lambda E: E.tensor_scalar(t2c[:, 3:4], t2c[:, 3:4], 2.0, None,
                                              ALU.mult),
                    reads=("t2c",), writes=("t2c",))
                p.v(lambda E, ltj=ltj: E.tensor_scalar(ltj, t2a[:, :], t2c[:, 3:4],
                                                       None, ALU.mult),
                    reads=("t2a", "t2c"), writes=("lt",))
            for o in range(OC):
                chunks = []
                for q in range(4):
                    entry = next_entry("b2")
                    assert entry["sel"] == (o, q)
                    dret = dma_chunk(entry, ps)
                    so = dret[1]
                    items = [((lambda so=so, kt=kt: wring[:, so + kt * 128:so + kt * 128 + 128]),
                              (lambda kt=kt, q=q: xbf[:, (q * FQ + kt) * TP:(q * FQ + kt + 1) * TP]),
                              ("xbf", q * FQ + kt)) for kt in range(FQ)]
                    chunks.append((dret, items))
                bank = (0, 1, 2, 3)[o % 4]
                tok = mm_group2(bank, chunks)
                j = o % 2
                p.s(lambda E, bank=bank, o=o, j=j: E.activation(
                    sg[j][:, :], pb[bank][:, :], AF.Identity, bias=b_s[:, o:o + 1]),
                    reads=(("pb", bank), "b"), writes=(("sg", j),))
                p.v(lambda E, o=o, j=j: E.tensor_tensor(
                    acc[:, o * TP:(o + 1) * TP], acc[:, o * TP:(o + 1) * TP],
                    sg[j][:, :], ALU.add),
                    reads=(("sg", j), ("acc", o)), writes=(("acc", o),))

            for j in range(NT):
                p.pre("tensor", reads=("lt", "ident"),
                      writes=(("pb", 7),) if j == 0 else ())
                def ft2(E, j=j):
                    return E.transpose(pb[7][0:NE, j * 128:(j + 1) * 128],
                                       lt[:, j * NE:(j + 1) * NE], ident_s[:, :])
                if j == NT - 1:
                    tok = p.emit("tensor", ft2, "TS")
                else:
                    p.emit("tensor", ft2, None)
            p.post(tok, reads=["lt"], writes=[("pb", 7)])
            p.v(lambda E: E.tensor_copy(wsb[:, :], pb[7][0:NE, :]),
                reads=(("pb", 7),), writes=("wsb",))
            p.pre("tensor", reads=("wsb", "rrep"), writes=(("pb", 6),))
            tok = p.emit("tensor", lambda E: E.matmul(pb[6][:, :], rrep_s[:, :],
                                                      wsb[:, :], start=True, stop=True),
                         "TS")
            p.post(tok, reads=["wsb", "rrep"], writes=[("pb", 6)])
            p.s(lambda E: E.copy(wrepsb[:, :], pb[6][:, :]),
                reads=(("pb", 6),), writes=("wrepsb",))
            p.v(lambda E: E.tensor_tensor(hbf[:, :], pb[5][:, :], wrepsb[:, :], ALU.mult),
                reads=(("pb", 5), "wrepsb"), writes=("hbf",))

            QW = OC // 4
            for q in range(4):
                el = next_entry("lorab")
                assert el["sel"] == q
                ldret = dma_chunk(el, ps)
                sol = ldret[1]
                for oo in range(QW):
                    o = q * QW + oo
                    bank = (0, 1, 2, 3)[o % 4]
                    items = [((lambda sol=sol, oo=oo: wring[:, sol + oo * 128:sol + oo * 128 + 128]),
                              (lambda: hbf[:, :]), "hbf")]
                    tok = mm_group2(bank, [(ldret if oo == 0 else None, items)])
                    p.post(tok, reads=[ldret[2]], writes=[])
                    p.v(lambda E, o=o, bank=bank: E.tensor_tensor(
                        acc[:, o * TP:(o + 1) * TP], acc[:, o * TP:(o + 1) * TP],
                        pb[bank][:, :], ALU.add),
                        reads=(("pb", bank), ("acc", o)), writes=(("acc", o),))

            layernorm(write_xbf=False)
            for g0 in range(0, OC, 4):
                p.o("sync", lambda E, t0=t0, g0=g0: E.dma_start(
                    outT.rearrange("(kc p) t -> p kc t", p=128)[:, g0:g0 + 4, t0:t0 + TP],
                    acc.rearrange("p (kc t) -> p kc t", kc=OC)[:, g0:g0 + 4, :]),
                    "XQ", reads=[("acc", cc) for cc in range(g0, g0 + 4)], n=16)
            assert plan_i[0] == len(plan)

        p.wait("sync", "XQ", p.cnt["XQ"])

        def replay(eng):
            def body(E):
                for item in p.prog[eng]:
                    if item[0] == "w":
                        E.wait_ge(SEM[item[1]], item[2])
                    else:
                        _, fn, inc, n = item
                        ins = fn(E)
                        if inc is not None:
                            ins.then_inc(SEM[inc], n)
            return body

        block.tensor(replay("tensor"))
        block.vector(replay("vector"))
        block.scalar(replay("scalar"))
        block.sync(replay("sync"))
        block.gpsimd(replay("gpsimd"))

    return nc, c, plan, total_cols


def stage_weights(c, plan, total_cols, inp):
    W = {
        "upT": np.ascontiguousarray(inp["L0_attn_up_w"].T),
        "gateT": np.ascontiguousarray(inp["L0_attn_gate_w"].T),
        "laoT": np.ascontiguousarray(inp["L0_attn_local_approx_output"].T),
        "fuseT": [np.ascontiguousarray(inp["L0_attn_fuse_weight"][:, :, m].T)
                  for m in range(c["G"])],
        "fgT": np.ascontiguousarray(inp["L0_attn_ffn_gate_w"].T),
        "fuT": np.ascontiguousarray(inp["L0_attn_ffn_up_w"].T),
        "fdT": np.ascontiguousarray(inp["L0_attn_ffn_down_w"].T),
        "wbT": np.ascontiguousarray(inp["L0_ffn_W_base"].T),
        "wrT": np.ascontiguousarray(inp["L0_ffn_W_router"].T),
        "lAT": np.ascontiguousarray(
            inp["L0_ffn_lora_A"].transpose(2, 0, 1).reshape(c["D"], c["E"] * c["R"])),
        "lBT": np.ascontiguousarray(
            inp["L0_ffn_lora_B"].transpose(0, 2, 1).reshape(c["E"] * c["R"], c["D"])),
    }
    stream = build_stream(c, plan, total_cols, W)
    smalls = {
        "lp_pack": np.ascontiguousarray(
            inp["L0_attn_local_point"].reshape(c["SC"], 128).T).astype(np.float32),
        "b_pack": np.ascontiguousarray(
            inp["L0_ffn_b_base"].reshape(c["OC"], 128).T).astype(np.float32),
        "disc_rep": np.ascontiguousarray(
            np.tile(inp["L0_attn_discount_factor"].reshape(1, c["G"]),
                    (128, 1))).astype(np.float32),
        "rrep": None,
        "ident": np.eye(128, dtype=np.float32),
        "ones_col": np.ones((1, 128), np.float32),
    }
    rrep = np.zeros((c["E"], 128), np.float32)
    for e in range(c["E"]):
        rrep[e, e * c["R"]:(e + 1) * c["R"]] = 1.0
    smalls["rrep"] = rrep
    return stream, smalls


def kernel(**inputs):
    inputs = {k: np.asarray(v, dtype=np.float32) for k, v in inputs.items()}
    cfg = real_cfg()
    c = derive(cfg)
    nc, c, plan, total_cols = build_core_program(cfg)
    stream, smalls = stage_weights(c, plan, total_cols, inputs)
    x = inputs["L0_attn_hidden_states"].reshape(c["TOKENS"], c["D"])
    in_maps = []
    for core in range(c["NCORES"]):
        xc = np.ascontiguousarray(
            x[core * c["TPC"]:(core + 1) * c["TPC"], :].T)
        m = {"xT": xc, "wstream": stream}
        m.update(smalls)
        in_maps.append(m)
    res = run_bass_kernel_spmd(nc, in_maps, core_ids=list(range(c["NCORES"])))
    outs = [r["outT"] for r in res.results]
    out = np.concatenate([o.T for o in outs], axis=0)
    return np.ascontiguousarray(
        out.reshape(2, c["TOKENS"] // 2, c["D"])).astype(np.float32)


# version 15
# speedup vs baseline: 1.0291x; 1.0096x over previous
import contextlib
import numpy as np
import concourse.bass as bass
import concourse.mybir as mybir
from concourse.bass_utils import run_bass_kernel_spmd

F32 = mybir.dt.float32
BF16 = mybir.dt.bfloat16
AF = mybir.ActivationFunctionType
ALU = mybir.AluOpType
AX = mybir.AxisListType

SEMNAMES = ("CQ", "XQ", "XB", "XC", "XD", "XE", "TS", "VS", "SS", "W0", "W1", "W2", "W3", "W4", "W5", "W6")
ENGS = ("tensor", "vector", "scalar", "sync", "gpsimd")


def real_cfg():
    return dict(D=4096, S=1024, FF=9984, E=8, R=16, TOKENS=8192, NCORES=8, TP=512,
                THD=2.5, EPS=1e-5, G=6)


def derive(cfg):
    c = dict(cfg)
    c["KC"] = c["D"] // 128
    c["SC"] = c["S"] // 128
    c["FC"] = c["FF"] // 128
    c["OC"] = c["D"] // 128
    assert c["E"] * c["R"] == 128
    secs, i = [], 0
    while i < c["FC"]:
        n = min(20, c["FC"] - i)
        secs.append(list(range(i, i + n)))
        i += n
    c["SECS"] = secs
    c["VSEC"] = max(len(s) for s in secs)
    c["NPASS"] = c["TOKENS"] // c["NCORES"] // c["TP"]
    c["TPC"] = c["TOKENS"] // c["NCORES"]
    c["NT"] = c["TP"] // 128
    return c


def make_plan(c):
    plan = []

    def add(kind, sel, cols, carve):
        plan.append({"kind": kind, "sel": sel, "cols": cols, "carve": carve})

    SH = c["SC"] // 2
    for h in range(2):
        for k in range(c["KC"]):
            add("upT_row", (k, h), SH * 128, "A")
    for h in range(2):
        for k in range(c["KC"]):
            add("gateT_row", (k, h), SH * 128, "A")
    for m in range(7):
        for o in range(c["OC"]):
            add("sterm", (m, o), c["SC"] * 128, "A")
    FH = c["KC"] // 2
    FQ = c["KC"] // 4
    for si, sec in enumerate(c["SECS"]):
        for f in sec:
            for gu in ("g", "u"):
                for q in range(4):
                    add("fgu", (gu, f, q), FQ * 128, "A")
        runs = []
        i0 = 0
        while i0 < len(sec):
            n = min(8, len(sec) - i0)
            runs.append((i0, n))
            i0 += n
        for o in range(c["OC"]):
            for ri, (r0, rn) in enumerate(runs):
                add("down", (si, o, ri, r0, rn), rn * 128, "A")
    add("router", None, c["KC"] * 8, "A")
    for q in range(4):
        add("lA", q, FQ * 128, "A")
    for o in range(c["OC"]):
        for q in range(4):
            add("b2", (o, q), FQ * 128, "A")
    for q in range(4):
        add("lorab", q, (c["OC"] // 4) * 128, "A")
    off = 0
    for p in plan:
        p["off"] = off
        off += p["cols"]
    return plan, off


def _img_tiles(block):
    n = block.shape[0] // 128
    w = block.shape[1]
    return np.ascontiguousarray(
        block.reshape(n, 128, w).transpose(1, 0, 2).reshape(128, n * w))


def build_stream(c, plan, total_cols, W):
    out = np.empty((128, total_cols), np.float32)
    for p in plan:
        k, sel, off, cols = p["kind"], p["sel"], p["off"], p["cols"]
        if k in ("upT_row", "gateT_row"):
            kk, h = sel
            srcw = W["upT"] if k == "upT_row" else W["gateT"]
            SH = srcw.shape[1] // 256
            img = srcw[kk * 128:(kk + 1) * 128, h * SH * 128:(h + 1) * SH * 128]
        elif k == "sterm":
            m, o = sel
            src = W["laoT"] if m == 0 else W["fuseT"][m - 1]
            img = _img_tiles(src[:, o * 128:(o + 1) * 128])
        elif k == "fgu":
            gu, f, q = sel
            srcw = W["fgT"] if gu == "g" else W["fuT"]
            FQ = srcw.shape[0] // 512
            img = _img_tiles(srcw[q * FQ * 128:(q + 1) * FQ * 128,
                                  f * 128:(f + 1) * 128])
        elif k == "down":
            si, o, ri, r0, rn = sel
            sec = c["SECS"][si]
            f0 = sec[0] + r0
            img = _img_tiles(W["fdT"][f0 * 128:(f0 + rn) * 128,
                                      o * 128:(o + 1) * 128])
        elif k == "router":
            img = _img_tiles(W["wrT"])
        elif k == "lA":
            q = sel
            FQ = W["lAT"].shape[0] // 512
            img = _img_tiles(W["lAT"][q * FQ * 128:(q + 1) * FQ * 128, :])
        elif k == "b2":
            o, q = sel
            FQ = W["wbT"].shape[0] // 512
            img = _img_tiles(W["wbT"][q * FQ * 128:(q + 1) * FQ * 128,
                                      o * 128:(o + 1) * 128])
        elif k == "lorab":
            q = sel
            qw = W["lBT"].shape[1] // 4
            img = W["lBT"][:, q * qw:(q + 1) * qw]
        else:
            raise KeyError(k)
        assert img.shape == (128, cols), (k, img.shape, cols)
        out[:, off:off + cols] = img
    return out


class Buf:
    __slots__ = ("w", "rs")

    def __init__(self):
        self.w = None
        self.rs = []


class P:
    def __init__(self):
        self.prog = {e: [] for e in ENGS}
        self.cnt = dict.fromkeys(SEMNAMES, 0)
        self.wm = {(e, s): 0 for e in ENGS for s in SEMNAMES}
        self.R = {}

    def wait(self, eng, sem, val):
        if val is None or val <= 0 or self.wm[(eng, sem)] >= val:
            return
        self.wm[(eng, sem)] = val
        self.prog[eng].append(("w", sem, val))

    def waitt(self, eng, tok):
        if tok is not None:
            self.wait(eng, tok[0], tok[1])

    def buf(self, key):
        b = self.R.get(key)
        if b is None:
            b = self.R[key] = Buf()
        return b

    def pre(self, eng, reads=(), writes=()):
        need = {}
        def want(t):
            if t is not None and t[1] > need.get(t[0], 0):
                need[t[0]] = t[1]
        for key in reads:
            want(self.buf(key).w)
        for key in writes:
            b = self.buf(key)
            want(b.w)
            for t in b.rs:
                want(t)
        for sem, val in need.items():
            self.wait(eng, sem, val)

    def post(self, tok, reads=(), writes=(), const_reads=()):
        for key in writes:
            b = self.buf(key)
            b.w = tok
            b.rs = []
        for key in reads:
            self.buf(key).rs.append(tok)

    def emit(self, eng, fn, inc=None, n=1):
        tok = None
        if inc is not None:
            self.cnt[inc] += n
            tok = (inc, self.cnt[inc])
        self.prog[eng].append(("o", fn, inc, n))
        return tok

    def o(self, eng, fn, inc, reads=(), writes=(), n=1):
        self.pre(eng, reads, writes)
        tok = self.emit(eng, fn, inc, n)
        self.post(tok, reads, writes)
        return tok

    def v(self, fn, reads=(), writes=()):
        return self.o("vector", fn, "VS", reads, writes)

    def s(self, fn, reads=(), writes=()):
        return self.o("scalar", fn, "SS", reads, writes)


def build_core_program(cfg):
    c = derive(cfg)
    D, S, FF, NE, R = c["D"], c["S"], c["FF"], c["E"], c["R"]
    KC, SC, FC, OC, TP = c["KC"], c["SC"], c["FC"], c["OC"], c["TP"]
    NPASS, TPC, NT = c["NPASS"], c["TPC"], c["NT"]
    THD, EPS = c["THD"], c["EPS"]
    plan, total_cols = make_plan(c)

    FH = KC // 2
    FQ = KC // 4
    CARVE = {"A": (max(e["cols"] for e in plan), 6)}
    ARENA = max(sc * ns for sc, ns in CARVE.values())

    nc = bass.Bass("TRN2", target_bir_lowering=False)
    xT = nc.dram_tensor("xT", [D, TPC], F32, kind="ExternalInput").ap()
    ws = nc.dram_tensor("wstream", [128, total_cols], F32, kind="ExternalInput").ap()
    lp_h = nc.dram_tensor("lp_pack", [128, SC], F32, kind="ExternalInput").ap()
    b_h = nc.dram_tensor("b_pack", [128, OC], F32, kind="ExternalInput").ap()
    disc_h = nc.dram_tensor("disc_rep", [128, 6], F32, kind="ExternalInput").ap()
    rrep_h = nc.dram_tensor("rrep", [NE, 128], F32, kind="ExternalInput").ap()
    ident_h = nc.dram_tensor("ident", [128, 128], F32, kind="ExternalInput").ap()
    onescol_h = nc.dram_tensor("ones_col", [1, 128], F32, kind="ExternalInput").ap()
    outT = nc.dram_tensor("outT", [D, TPC], F32, kind="ExternalOutput").ap()

    st = contextlib.ExitStack()
    sb = lambda nm, shp, dt: st.enter_context(nc.sbuf_tensor(nm, shp, dt))
    with st:
        acc = sb("acc", [128, OC * TP], F32)
        xbf = sb("xbf", [128, KC * TP], BF16)
        aubf = sb("aubf", [128, SC * TP], BF16)
        delta = sb("delta", [128, SC * TP], F32)
        pw = sb("pw", [128, SC * TP], F32)
        tmpb = [sb("tmp0", [128, SC * TP], BF16), sb("tmp1", [128, SC * TP], BF16)]
        vsec = sb("vsec", [128, c["VSEC"] * TP], BF16)
        wring = sb("wring", [128, ARENA], BF16)
        lp_s = sb("lp_s", [128, SC], F32)
        lpn_s = sb("lpn_s", [128, SC], F32)
        b_s = sb("b_s", [128, OC], F32)
        crep = sb("crep", [128, 6], F32)
        rrep_s = sb("rrep_s", [NE, 128], F32)
        ident_s = sb("ident_s", [128, 128], F32)
        onescol_s = sb("ones_col_s", [1, 128], F32)
        onesk_s = sb("onesk_s", [128, 1], BF16)
        sg = [sb("sgA", [128, TP], F32), sb("sgB", [128, TP], F32)]
        statb = [sb("statA", [128, TP], BF16), sb("statB", [128, TP], BF16)]
        sqb = [sb("sqA", [128, TP], BF16), sb("sqB", [128, TP], BF16)]
        rows = sb("rows", [1, 3 * TP], F32)
        lnt = sb("lnt", [128, TP], F32)
        lt = sb("lt", [128, NT * NE], F32)
        t2a = sb("t2a", [128, NE], F32)
        t2b = sb("t2b", [128, NE], F32)
        t2c = sb("t2c", [128, 4], F32)
        wsb = sb("wsb", [NE, TP], F32)
        wrepsb = sb("wrepsb", [128, TP], F32)
        hbf = sb("hbf", [128, TP], BF16)
        pb = [st.enter_context(nc.psum_tensor("pb%d" % i, [128, TP], F32))
              for i in range(8)]
        SEM = {nm: st.enter_context(nc.semaphore(nm)) for nm in SEMNAMES}
        block = st.enter_context(nc.Block())

        p = P()
        ACCK = [("acc", o) for o in range(OC)]
        XBFK = [("xbf", k) for k in range(KC)]

        plan_by_carve_pos = {}
        cur_carve = [None]
        carve_keys = [set()]

        def dma_chunk(entry, pass_i):
            carve = entry["carve"]
            cols, nslot = CARVE[carve]
            if cur_carve[0] != carve:
                for key in carve_keys[0]:
                    b = p.buf(key)
                    p.waitt("gpsimd", b.w)
                    for t in b.rs:
                        p.waitt("gpsimd", t)
                    del p.R[key]
                cur_carve[0] = carve
                carve_keys[0] = set()
                plan_by_carve_pos[carve] = 0
            i = plan_by_carve_pos[carve]
            plan_by_carve_pos[carve] = i + 1
            slot = i % nslot
            so = slot * cols
            key = ("slot", carve, slot)
            carve_keys[0].add(key)
            ccols = entry["cols"]
            off = entry["off"]
            sem = "W%d" % slot
            tok = p.o("gpsimd",
                      lambda E, off=off, ccols=ccols, so=so:
                      E.dma_start(wring[:, so:so + ccols], ws[:, off:off + ccols]),
                      sem, writes=(key,), n=16)
            return tok, so, key

        def mm_group2(bank, chunks, extra_reads=()):
            p.pre("tensor", reads=(), writes=(("pb", bank),))
            tot = sum(len(it) for _, it in chunks)
            i = 0
            tok = None
            for dret, items in chunks:
                need = {}
                if dret is not None:
                    need[dret[0][0]] = max(need.get(dret[0][0], 0), dret[0][1])
                mov_keys = set()
                for sf, mf, mk in items:
                    if mk is not None:
                        mov_keys.add(mk)
                        t = p.buf(mk).w
                        if t is not None:
                            need[t[0]] = max(need.get(t[0], 0), t[1])
                for sem, val in need.items():
                    p.wait("tensor", sem, val)
                n_it = len(items)
                for j, (sf, mf, mk) in enumerate(items):
                    def fn(E, b=bank, sf=sf, mf=mf, st_=(i == 0), sp=(i == tot - 1)):
                        return E.matmul(pb[b][:, :], sf(), mf(), start=st_, stop=sp)
                    if j == n_it - 1:
                        tok = p.emit("tensor", fn, "TS")
                    else:
                        p.emit("tensor", fn, None)
                    i += 1
                rd = list(mov_keys)
                if dret is not None:
                    rd.append(dret[2])
                p.post(tok, reads=rd, writes=[])
            p.post(tok, reads=list(extra_reads), writes=[("pb", bank)])
            return tok

        const_keys = []
        for ap_s, ap_h, key in ((lp_s, lp_h, "lp"), (b_s, b_h, "b"),
                                (crep, disc_h, "crep"), (rrep_s, rrep_h, "rrep"),
                                (ident_s, ident_h, "ident"),
                                (onescol_s, onescol_h, "onescol")):
            p.emit("sync", lambda E, d=ap_s, s_=ap_h: E.dma_start(d[:, :], s_[:, :]),
                   "CQ", 16)
            const_keys.append(key)
        ctok = ("CQ", p.cnt["CQ"])
        for key in const_keys:
            p.buf(key).w = ctok
        p.s(lambda E: E.activation(crep[:, :], crep[:, :], AF.Exp, scale=-1.0),
            reads=("crep",), writes=("crep",))
        p.v(lambda E: E.tensor_scalar(lpn_s[:, :], lp_s[:, :], -1.0, None, ALU.mult),
            reads=("lp",), writes=("lpn",))
        p.v(lambda E: E.tensor_scalar(onesk_s[:, :], b_s[:, 0:1], 0.0, 1.0,
                                      ALU.mult, ALU.add),
            reads=("b",), writes=("onesk",))

        for ps in range(NPASS):
            t0 = ps * TP
            p.o("sync", lambda E, t0=t0: E.dma_start(
                acc.rearrange("p (kc t) -> p kc t", kc=OC),
                xT.rearrange("(kc p) t -> p kc t", p=128)[:, :, t0:t0 + TP]),
                "XQ", writes=ACCK, n=16)
            QK = KC // 4

            def emit_xbf_quarter(qi, t0=t0, QK=QK):
                qsem = ("XB", "XC", "XD", "XE")[qi]
                p.o("gpsimd", lambda E, t0=t0, QK=QK, qi=qi: E.dma_start(
                    xbf.rearrange("p (kc t) -> p kc t", kc=KC)[:, qi * QK:(qi + 1) * QK, :],
                    xT.rearrange("(kc p) t -> p kc t", p=128)[:, qi * QK:(qi + 1) * QK, t0:t0 + TP]),
                    qsem, writes=XBFK[qi * QK:(qi + 1) * QK], n=16)

            plan_i = [0]

            def next_entry(expect_kind=None):
                e = plan[plan_i[0]]
                plan_i[0] += 1
                if expect_kind is not None:
                    assert e["kind"] == expect_kind, (e, expect_kind)
                return e

            SH = SC // 2
            for sub, kind in (("a", "upT_row"), ("b", "gateT_row")):
                for h in range(2):
                    for k in range(KC):
                        if sub == "a" and h == 0 and k % QK == 0 and (
                                ps > 0 or k // QK < 2):
                            emit_xbf_quarter(k // QK)
                        if ps == 0 and sub == "a" and h == 0 and k >= 2 * QK \
                                and (k - 2 * QK) % 2 == 0:
                            for kcc in (k, k + 1):
                                if kcc < KC:
                                    p.s(lambda E, kcc=kcc: E.copy(
                                        xbf[:, kcc * TP:(kcc + 1) * TP],
                                        acc[:, kcc * TP:(kcc + 1) * TP]),
                                        reads=(("acc", kcc),), writes=(("xbf", kcc),))
                        entry = next_entry(kind)
                        assert entry["sel"] == (k, h)
                        wtok, so, skey = dma_chunk(entry, ps)
                        p.waitt("tensor", wtok)
                        for sc in range(SH):
                            if k == 0:
                                p.pre("tensor", writes=(("pb", sc),))
                            p.waitt("tensor", p.buf(("xbf", k)).w)
                            def fn(E, sc=sc, so=so, k=k, st_=(k == 0), sp=(k == KC - 1)):
                                return E.matmul(pb[sc][:, :],
                                                wring[:, so + sc * 128:so + sc * 128 + 128],
                                                xbf[:, k * TP:(k + 1) * TP],
                                                start=st_, stop=sp)
                            if sc == SH - 1:
                                tok = p.emit("tensor", fn, "TS")
                            else:
                                p.emit("tensor", fn, None)
                        p.post(tok, reads=[skey, ("xbf", k)],
                               writes=[("pb", sc2) for sc2 in range(SH)]
                               if k == KC - 1 else [])
                    for sc in range(SH):
                        scg = h * SH + sc
                        if sub == "a":
                            p.s(lambda E, sc=sc, scg=scg: E.copy(
                                aubf[:, scg * TP:(scg + 1) * TP], pb[sc][:, :]),
                                reads=(("pb", sc),), writes=(("aubf", scg),))
                        else:
                            p.s(lambda E, sc=sc, scg=scg: E.activation(
                                delta[:, scg * TP:(scg + 1) * TP], pb[sc][:, :],
                                AF.Identity, bias=lpn_s[:, scg:scg + 1]),
                                reads=(("pb", sc), "lpn"), writes=(("delta", scg),))

            def emit_tmp_chunk(m, sc):
                bi = (m - 1) % 2
                sl = slice(sc * TP, (sc + 1) * TP)
                sgi = sc % 2
                if m == 2:
                    p.v(lambda E, sl=sl: E.tensor_tensor(pw[:, sl], delta[:, sl],
                                                         delta[:, sl], ALU.mult),
                        reads=(("delta", sc),), writes=(("pw", sc),))
                elif m >= 3:
                    p.v(lambda E, sl=sl: E.tensor_tensor(pw[:, sl], pw[:, sl],
                                                         delta[:, sl], ALU.mult),
                        reads=(("pw", sc), ("delta", sc)), writes=(("pw", sc),))
                if m == 5:
                    p.v(lambda E, sl=sl: E.tensor_scalar(lnt[:, :], delta[:, sl],
                                                         THD, None, ALU.is_le),
                        reads=(("delta", sc),), writes=("lnt",))
                    p.v(lambda E, sl=sl: E.tensor_tensor(pw[:, sl], pw[:, sl],
                                                         lnt[:, :], ALU.mult),
                        reads=(("pw", sc), "lnt"), writes=(("pw", sc),))
                    p.v(lambda E, sl=sl: E.tensor_scalar(lnt[:, :], delta[:, sl],
                                                         -THD, None, ALU.is_ge),
                        reads=(("delta", sc),), writes=("lnt",))
                    p.v(lambda E, sl=sl: E.tensor_tensor(pw[:, sl], pw[:, sl],
                                                         lnt[:, :], ALU.mult),
                        reads=(("pw", sc), "lnt"), writes=(("pw", sc),))
                base, bkey = (delta, ("delta", sc)) if m == 1 else (pw, ("pw", sc))
                p.v(lambda E, sl=sl, m=m, base=base, bi=bi: E.scalar_tensor_tensor(
                    tmpb[bi][:, sl], base[:, sl], crep[:, m - 1:m], aubf[:, sl],
                    ALU.mult, ALU.mult),
                    reads=(bkey, "crep", ("aubf", sc)), writes=(("tmp", bi, sc),))

            for m in range(7):
                if m == 0:
                    movbuf, movkey = aubf, lambda kk: ("aubf", kk)
                else:
                    bi = (m - 1) % 2
                    movbuf, movkey = tmpb[bi], (lambda kk, bi=bi: ("tmp", bi, kk))
                for o in range(OC):
                    entry = next_entry("sterm")
                    assert entry["sel"] == (m, o)
                    dret = dma_chunk(entry, ps)
                    so = dret[1]
                    bank = (0, 1, 2, 3)[o % 4]
                    items = [((lambda so=so, kk=kk: wring[:, so + kk * 128:so + kk * 128 + 128]),
                              (lambda kk=kk, movbuf=movbuf: movbuf[:, kk * TP:(kk + 1) * TP]),
                              movkey(kk)) for kk in range(SC)]
                    tok = mm_group2(bank, [(dret, items)])
                    p.v(lambda E, o=o, bank=bank: E.tensor_tensor(
                        acc[:, o * TP:(o + 1) * TP], acc[:, o * TP:(o + 1) * TP],
                        pb[bank][:, :], ALU.add),
                        reads=(("pb", bank), ("acc", o)), writes=(("acc", o),))
                    if m < 6 and o % 2 == 0 and o // 2 < SC:
                        emit_tmp_chunk(m + 1, o // 2)

            fi = 0
            for si, sec in enumerate(c["SECS"]):
                for li, f in enumerate(sec):
                    gb = 2 + 2 * (fi % 2)
                    ub = 3 + 2 * (fi % 2)
                    for gu, bank in (("g", gb), ("u", ub)):
                        chunks = []
                        for q in range(4):
                            e_ = next_entry("fgu")
                            assert e_["sel"] == (gu, f, q)
                            dret = dma_chunk(e_, ps)
                            so = dret[1]
                            items = [((lambda so=so, kk=kk: wring[:, so + kk * 128:so + kk * 128 + 128]),
                                      (lambda kk=kk, q=q: xbf[:, (q * FQ + kk) * TP:(q * FQ + kk + 1) * TP]),
                                      ("xbf", q * FQ + kk)) for kk in range(FQ)]
                            chunks.append((dret, items))
                        mm_group2(bank, chunks)
                    sgi = fi % 2
                    p.s(lambda E, gb=gb, sgi=sgi: E.activation(
                        sg[sgi][:, :], pb[gb][:, :], AF.Silu),
                        reads=(("pb", gb),), writes=(("sg", sgi),))
                    p.v(lambda E, li=li, ub=ub, sgi=sgi: E.tensor_tensor(
                        vsec[:, li * TP:(li + 1) * TP], sg[sgi][:, :],
                        pb[ub][:, :], ALU.mult),
                        reads=(("sg", sgi), ("pb", ub)), writes=(("vsec", li),))
                    fi += 1
                runs = []
                i0 = 0
                while i0 < len(sec):
                    n = min(8, len(sec) - i0)
                    runs.append((i0, n))
                    i0 += n
                for o in range(OC):
                    chunks = []
                    for ri, (r0, rn) in enumerate(runs):
                        entry = next_entry("down")
                        assert entry["sel"] == (si, o, ri, r0, rn)
                        dret = dma_chunk(entry, ps)
                        so = dret[1]
                        items = [((lambda so=so, kk=kk: wring[:, so + kk * 128:so + kk * 128 + 128]),
                                  (lambda kk=kk, r0=r0: vsec[:, (r0 + kk) * TP:(r0 + kk + 1) * TP]),
                                  ("vsec", r0 + kk)) for kk in range(rn)]
                        chunks.append((dret, items))
                    bank = (0, 1, 6, 7)[o % 4]
                    tok = mm_group2(bank, chunks)
                    p.v(lambda E, o=o, bank=bank: E.tensor_tensor(
                        acc[:, o * TP:(o + 1) * TP], acc[:, o * TP:(o + 1) * TP],
                        pb[bank][:, :], ALU.add),
                        reads=(("pb", bank), ("acc", o)), writes=(("acc", o),))

            def layernorm(write_xbf):
                for cc in range(OC):
                    j = cc % 2
                    p.v(lambda E, cc=cc, j=j: E.tensor_copy(
                        statb[j][:, :], acc[:, cc * TP:(cc + 1) * TP]),
                        reads=(("acc", cc),), writes=(("statb", j),))
                    p.s(lambda E, cc=cc, j=j: E.activation(
                        sqb[j][:, :], acc[:, cc * TP:(cc + 1) * TP], AF.Square),
                        reads=(("acc", cc),), writes=(("sqb", j),))
                    p.pre("tensor", reads=(("statb", j), "onesk"),
                          writes=(("pb", 4),) if cc == 0 else ())
                    if cc == 0:
                        p.pre("tensor", writes=(("pb", 5),))
                    def f1(E, j=j, st_=(cc == 0), sp=(cc == OC - 1)):
                        return E.matmul(pb[4][0:1, :], onesk_s[:, :],
                                        statb[j][:, :], start=st_, stop=sp)
                    p.emit("tensor", f1, None)
                    p.pre("tensor", reads=(("sqb", j),))
                    def f2(E, j=j, st_=(cc == 0), sp=(cc == OC - 1)):
                        return E.matmul(pb[5][0:1, :], onesk_s[:, :],
                                        sqb[j][:, :], start=st_, stop=sp)
                    tok = p.emit("tensor", f2, "TS")
                    p.post(tok, reads=[("statb", j), ("sqb", j)],
                           writes=[("pb", 4), ("pb", 5)] if cc == OC - 1 else [])
                r0 = rows[0:1, 0:TP]
                r1 = rows[0:1, TP:2 * TP]
                r2 = rows[0:1, 2 * TP:3 * TP]
                p.v(lambda E: E.tensor_scalar(r0, pb[4][0:1, :], 1.0 / D, None, ALU.mult),
                    reads=(("pb", 4),), writes=("r0",))
                p.v(lambda E: E.tensor_scalar(r1, pb[5][0:1, :], 1.0 / D, None, ALU.mult),
                    reads=(("pb", 5),), writes=("r1",))
                p.v(lambda E: E.tensor_tensor(r2, r0, r0, ALU.mult),
                    reads=("r0",), writes=("r2",))
                p.v(lambda E: E.tensor_tensor(r1, r1, r2, ALU.subtract),
                    reads=("r1", "r2"), writes=("r1",))
                p.v(lambda E: E.tensor_scalar(r1, r1, EPS, None, ALU.add),
                    reads=("r1",), writes=("r1",))
                p.v(lambda E: E.reciprocal(r1, r1), reads=("r1",), writes=("r1",))
                p.s(lambda E: E.activation(r1, r1, AF.Sqrt), reads=("r1",), writes=("r1",))
                p.v(lambda E: E.tensor_tensor(r2, r0, r1, ALU.mult),
                    reads=("r0", "r1"), writes=("r2",))
                p.pre("tensor", reads=("r1", "onescol"), writes=(("pb", 6),))
                p.emit("tensor", lambda E: E.matmul(pb[6][:, :], onescol_s[:, :], r1,
                                                    start=True, stop=True), "TS")
                p.pre("tensor", reads=("r2",), writes=(("pb", 7),))
                tok = p.emit("tensor", lambda E: E.matmul(pb[7][:, :], onescol_s[:, :],
                                                          r2, start=True, stop=True), "TS")
                p.post(tok, reads=["r1", "r2"], writes=[("pb", 6), ("pb", 7)])
                for cc in range(OC):
                    p.v(lambda E, cc=cc: E.tensor_tensor(
                        lnt[:, :], acc[:, cc * TP:(cc + 1) * TP], pb[6][:, :], ALU.mult),
                        reads=(("acc", cc), ("pb", 6)), writes=("lnt",))
                    p.v(lambda E, cc=cc: E.tensor_tensor(
                        acc[:, cc * TP:(cc + 1) * TP], lnt[:, :], pb[7][:, :],
                        ALU.subtract),
                        reads=("lnt", ("pb", 7)), writes=(("acc", cc),))
                    if write_xbf:
                        p.s(lambda E, cc=cc: E.copy(xbf[:, cc * TP:(cc + 1) * TP],
                                                    acc[:, cc * TP:(cc + 1) * TP]),
                            reads=(("acc", cc),), writes=(("xbf", cc),))

            layernorm(write_xbf=True)

            er = next_entry("router")
            wtr, sor, skr = dma_chunk(er, ps)
            p.waitt("tensor", wtr)
            p.pre("tensor", writes=(("pb", 4),))
            for kt in range(KC):
                p.waitt("tensor", p.buf(("xbf", kt)).w)
                def fl(E, kt=kt, sor=sor, st_=(kt == 0), sp=(kt == KC - 1)):
                    return E.matmul(pb[4][0:NE, :], wring[:, sor + kt * 8:sor + kt * 8 + 8],
                                    xbf[:, kt * TP:(kt + 1) * TP], start=st_, stop=sp)
                if kt == KC - 1:
                    tok = p.emit("tensor", fl, "TS")
                else:
                    p.emit("tensor", fl, None)
            p.post(tok, reads=[skr] + XBFK, writes=[("pb", 4)])
            hchunks = []
            for q in range(4):
                ea = next_entry("lA")
                assert ea["sel"] == q
                dret = dma_chunk(ea, ps)
                soa = dret[1]
                items = [((lambda soa=soa, kt=kt: wring[:, soa + kt * 128:soa + kt * 128 + 128]),
                          (lambda kt=kt, q=q: xbf[:, (q * FQ + kt) * TP:(q * FQ + kt + 1) * TP]),
                          ("xbf", q * FQ + kt)) for kt in range(FQ)]
                hchunks.append((dret, items))
            htok = mm_group2(5, hchunks)
            p.v(lambda E: E.tensor_copy(wsb[:, :], pb[4][0:NE, :]),
                reads=(("pb", 4),), writes=("wsb",))
            for j in range(NT):
                p.pre("tensor", reads=("wsb", "ident"),
                      writes=(("pb", 7),) if j == 0 else ())
                def ft(E, j=j):
                    return E.transpose(pb[7][:, j * NE:(j + 1) * NE],
                                       wsb[0:NE, j * 128:(j + 1) * 128],
                                       ident_s[0:NE, 0:NE])
                if j == NT - 1:
                    tok = p.emit("tensor", ft, "TS")
                else:
                    p.emit("tensor", ft, None)
            p.post(tok, reads=["wsb"], writes=[("pb", 7)])
            p.v(lambda E: E.tensor_copy(lt[:, :], pb[7][:, 0:NT * NE]),
                reads=(("pb", 7),), writes=("lt",))
            for j in range(NT):
                ltj = lt[:, j * NE:(j + 1) * NE]
                p.v(lambda E, ltj=ltj: E.reduce_max(t2c[:, 0:1], ltj, axis=AX.X),
                    reads=("lt",), writes=("t2c",))
                p.v(lambda E: E.tensor_scalar(t2c[:, 1:2], t2c[:, 0:1], -1.0, None,
                                              ALU.mult),
                    reads=("t2c",), writes=("t2c",))
                p.v(lambda E, ltj=ltj: E.tensor_scalar(t2a[:, :], ltj, t2c[:, 0:1],
                                                       None, ALU.is_equal),
                    reads=("lt", "t2c"), writes=("t2a",))
                p.v(lambda E, ltj=ltj: E.scalar_tensor_tensor(
                    t2b[:, :], t2a[:, :], -1e30, ltj, ALU.mult, ALU.add),
                    reads=("t2a", "lt"), writes=("t2b",))
                p.v(lambda E: E.reduce_max(t2c[:, 2:3], t2b[:, :], axis=AX.X),
                    reads=("t2b",), writes=("t2c",))
                p.v(lambda E: E.tensor_scalar(t2b[:, :], t2b[:, :], t2c[:, 2:3],
                                              None, ALU.is_equal),
                    reads=("t2b", "t2c"), writes=("t2b",))
                p.v(lambda E: E.tensor_tensor(t2a[:, :], t2a[:, :], t2b[:, :], ALU.add),
                    reads=("t2a", "t2b"), writes=("t2a",))
                p.s(lambda E, ltj=ltj: E.activation(t2b[:, :], ltj, AF.Exp,
                                                    bias=t2c[:, 1:2]),
                    reads=("lt", "t2c"), writes=("t2b",))
                p.v(lambda E: E.tensor_tensor(t2a[:, :], t2b[:, :], t2a[:, :], ALU.mult),
                    reads=("t2b", "t2a"), writes=("t2a",))
                p.v(lambda E: E.reduce_sum(t2c[:, 3:4], t2a[:, :], axis=AX.X),
                    reads=("t2a",), writes=("t2c",))
                p.v(lambda E: E.reciprocal(t2c[:, 3:4], t2c[:, 3:4]),
                    reads=("t2c",), writes=("t2c",))
                p.v(lambda E: E.tensor_scalar(t2c[:, 3:4], t2c[:, 3:4], 2.0, None,
                                              ALU.mult),
                    reads=("t2c",), writes=("t2c",))
                p.v(lambda E, ltj=ltj: E.tensor_scalar(ltj, t2a[:, :], t2c[:, 3:4],
                                                       None, ALU.mult),
                    reads=("t2a", "t2c"), writes=("lt",))
            for o in range(OC):
                chunks = []
                for q in range(4):
                    entry = next_entry("b2")
                    assert entry["sel"] == (o, q)
                    dret = dma_chunk(entry, ps)
                    so = dret[1]
                    items = [((lambda so=so, kt=kt: wring[:, so + kt * 128:so + kt * 128 + 128]),
                              (lambda kt=kt, q=q: xbf[:, (q * FQ + kt) * TP:(q * FQ + kt + 1) * TP]),
                              ("xbf", q * FQ + kt)) for kt in range(FQ)]
                    chunks.append((dret, items))
                bank = (0, 1, 2, 3)[o % 4]
                tok = mm_group2(bank, chunks)
                p.v(lambda E, bank=bank, o=o: E.scalar_tensor_tensor(
                    acc[:, o * TP:(o + 1) * TP], pb[bank][:, :], b_s[:, o:o + 1],
                    acc[:, o * TP:(o + 1) * TP], ALU.add, ALU.add),
                    reads=(("pb", bank), "b", ("acc", o)), writes=(("acc", o),))

            for j in range(NT):
                p.pre("tensor", reads=("lt", "ident"),
                      writes=(("pb", 7),) if j == 0 else ())
                def ft2(E, j=j):
                    return E.transpose(pb[7][0:NE, j * 128:(j + 1) * 128],
                                       lt[:, j * NE:(j + 1) * NE], ident_s[:, :])
                if j == NT - 1:
                    tok = p.emit("tensor", ft2, "TS")
                else:
                    p.emit("tensor", ft2, None)
            p.post(tok, reads=["lt"], writes=[("pb", 7)])
            p.v(lambda E: E.tensor_copy(wsb[:, :], pb[7][0:NE, :]),
                reads=(("pb", 7),), writes=("wsb",))
            p.pre("tensor", reads=("wsb", "rrep"), writes=(("pb", 6),))
            tok = p.emit("tensor", lambda E: E.matmul(pb[6][:, :], rrep_s[:, :],
                                                      wsb[:, :], start=True, stop=True),
                         "TS")
            p.post(tok, reads=["wsb", "rrep"], writes=[("pb", 6)])
            p.s(lambda E: E.copy(wrepsb[:, :], pb[6][:, :]),
                reads=(("pb", 6),), writes=("wrepsb",))
            p.v(lambda E: E.tensor_tensor(hbf[:, :], pb[5][:, :], wrepsb[:, :], ALU.mult),
                reads=(("pb", 5), "wrepsb"), writes=("hbf",))

            QW = OC // 4
            for q in range(4):
                el = next_entry("lorab")
                assert el["sel"] == q
                ldret = dma_chunk(el, ps)
                sol = ldret[1]
                for oo in range(QW):
                    o = q * QW + oo
                    bank = (0, 1, 2, 3)[o % 4]
                    items = [((lambda sol=sol, oo=oo: wring[:, sol + oo * 128:sol + oo * 128 + 128]),
                              (lambda: hbf[:, :]), "hbf")]
                    tok = mm_group2(bank, [(ldret if oo == 0 else None, items)])
                    p.post(tok, reads=[ldret[2]], writes=[])
                    p.v(lambda E, o=o, bank=bank: E.tensor_tensor(
                        acc[:, o * TP:(o + 1) * TP], acc[:, o * TP:(o + 1) * TP],
                        pb[bank][:, :], ALU.add),
                        reads=(("pb", bank), ("acc", o)), writes=(("acc", o),))

            layernorm(write_xbf=False)
            for g0 in range(0, OC, 4):
                p.o("sync", lambda E, t0=t0, g0=g0: E.dma_start(
                    outT.rearrange("(kc p) t -> p kc t", p=128)[:, g0:g0 + 4, t0:t0 + TP],
                    acc.rearrange("p (kc t) -> p kc t", kc=OC)[:, g0:g0 + 4, :]),
                    "XQ", reads=[("acc", cc) for cc in range(g0, g0 + 4)], n=16)
            assert plan_i[0] == len(plan)

        p.wait("sync", "XQ", p.cnt["XQ"])

        def replay(eng):
            def body(E):
                for item in p.prog[eng]:
                    if item[0] == "w":
                        E.wait_ge(SEM[item[1]], item[2])
                    else:
                        _, fn, inc, n = item
                        ins = fn(E)
                        if inc is not None:
                            ins.then_inc(SEM[inc], n)
            return body

        block.tensor(replay("tensor"))
        block.vector(replay("vector"))
        block.scalar(replay("scalar"))
        block.sync(replay("sync"))
        block.gpsimd(replay("gpsimd"))

    return nc, c, plan, total_cols


def stage_weights(c, plan, total_cols, inp):
    W = {
        "upT": np.ascontiguousarray(inp["L0_attn_up_w"].T),
        "gateT": np.ascontiguousarray(inp["L0_attn_gate_w"].T),
        "laoT": np.ascontiguousarray(inp["L0_attn_local_approx_output"].T),
        "fuseT": [np.ascontiguousarray(inp["L0_attn_fuse_weight"][:, :, m].T)
                  for m in range(c["G"])],
        "fgT": np.ascontiguousarray(inp["L0_attn_ffn_gate_w"].T),
        "fuT": np.ascontiguousarray(inp["L0_attn_ffn_up_w"].T),
        "fdT": np.ascontiguousarray(inp["L0_attn_ffn_down_w"].T),
        "wbT": np.ascontiguousarray(inp["L0_ffn_W_base"].T),
        "wrT": np.ascontiguousarray(inp["L0_ffn_W_router"].T),
        "lAT": np.ascontiguousarray(
            inp["L0_ffn_lora_A"].transpose(2, 0, 1).reshape(c["D"], c["E"] * c["R"])),
        "lBT": np.ascontiguousarray(
            inp["L0_ffn_lora_B"].transpose(0, 2, 1).reshape(c["E"] * c["R"], c["D"])),
    }
    stream = build_stream(c, plan, total_cols, W)
    smalls = {
        "lp_pack": np.ascontiguousarray(
            inp["L0_attn_local_point"].reshape(c["SC"], 128).T).astype(np.float32),
        "b_pack": np.ascontiguousarray(
            inp["L0_ffn_b_base"].reshape(c["OC"], 128).T).astype(np.float32),
        "disc_rep": np.ascontiguousarray(
            np.tile(inp["L0_attn_discount_factor"].reshape(1, c["G"]),
                    (128, 1))).astype(np.float32),
        "rrep": None,
        "ident": np.eye(128, dtype=np.float32),
        "ones_col": np.ones((1, 128), np.float32),
    }
    rrep = np.zeros((c["E"], 128), np.float32)
    for e in range(c["E"]):
        rrep[e, e * c["R"]:(e + 1) * c["R"]] = 1.0
    smalls["rrep"] = rrep
    return stream, smalls


def kernel(**inputs):
    inputs = {k: np.asarray(v, dtype=np.float32) for k, v in inputs.items()}
    cfg = real_cfg()
    c = derive(cfg)
    nc, c, plan, total_cols = build_core_program(cfg)
    stream, smalls = stage_weights(c, plan, total_cols, inputs)
    x = inputs["L0_attn_hidden_states"].reshape(c["TOKENS"], c["D"])
    in_maps = []
    for core in range(c["NCORES"]):
        xc = np.ascontiguousarray(
            x[core * c["TPC"]:(core + 1) * c["TPC"], :].T)
        m = {"xT": xc, "wstream": stream}
        m.update(smalls)
        in_maps.append(m)
    res = run_bass_kernel_spmd(nc, in_maps, core_ids=list(range(c["NCORES"])))
    outs = [r["outT"] for r in res.results]
    out = np.concatenate([o.T for o in outs], axis=0)
    return np.ascontiguousarray(
        out.reshape(2, c["TOKENS"] // 2, c["D"])).astype(np.float32)


# version 16
# speedup vs baseline: 1.0341x; 1.0049x over previous
import contextlib
import numpy as np
import concourse.bass as bass
import concourse.mybir as mybir
from concourse.bass_utils import run_bass_kernel_spmd

F32 = mybir.dt.float32
BF16 = mybir.dt.bfloat16
AF = mybir.ActivationFunctionType
ALU = mybir.AluOpType
AX = mybir.AxisListType

SEMNAMES = ("CQ", "XQ", "XB", "XC", "XD", "XE", "XF", "XG", "XH", "XI", "TS", "VS", "SS", "W0", "W1", "W2", "W3", "W4", "W5", "W6")
ENGS = ("tensor", "vector", "scalar", "sync", "gpsimd")


def real_cfg():
    return dict(D=4096, S=1024, FF=9984, E=8, R=16, TOKENS=8192, NCORES=8, TP=512,
                THD=2.5, EPS=1e-5, G=6)


def derive(cfg):
    c = dict(cfg)
    c["KC"] = c["D"] // 128
    c["SC"] = c["S"] // 128
    c["FC"] = c["FF"] // 128
    c["OC"] = c["D"] // 128
    assert c["E"] * c["R"] == 128
    secs, i = [], 0
    while i < c["FC"]:
        n = min(20, c["FC"] - i)
        secs.append(list(range(i, i + n)))
        i += n
    c["SECS"] = secs
    c["VSEC"] = max(len(s) for s in secs)
    c["NPASS"] = c["TOKENS"] // c["NCORES"] // c["TP"]
    c["TPC"] = c["TOKENS"] // c["NCORES"]
    c["NT"] = c["TP"] // 128
    return c


def make_plan(c):
    plan = []

    def add(kind, sel, cols, carve):
        plan.append({"kind": kind, "sel": sel, "cols": cols, "carve": carve})

    SH = c["SC"] // 2
    for h in range(2):
        for k in range(c["KC"]):
            add("upT_row", (k, h), SH * 128, "A")
    for h in range(2):
        for k in range(c["KC"]):
            add("gateT_row", (k, h), SH * 128, "A")
    for m in range(7):
        for o in range(c["OC"]):
            add("sterm", (m, o), c["SC"] * 128, "A")
    FH = c["KC"] // 2
    FQ = c["KC"] // 4
    for si, sec in enumerate(c["SECS"]):
        for f in sec:
            for gu in ("g", "u"):
                for q in range(4):
                    add("fgu", (gu, f, q), FQ * 128, "A")
        runs = []
        i0 = 0
        while i0 < len(sec):
            n = min(8, len(sec) - i0)
            runs.append((i0, n))
            i0 += n
        for o in range(c["OC"]):
            for ri, (r0, rn) in enumerate(runs):
                add("down", (si, o, ri, r0, rn), rn * 128, "A")
    add("router", None, c["KC"] * 8, "A")
    for q in range(4):
        add("lA", q, FQ * 128, "A")
    for o in range(c["OC"]):
        for q in range(4):
            add("b2", (o, q), FQ * 128, "A")
    for q in range(4):
        add("lorab", q, (c["OC"] // 4) * 128, "A")
    off = 0
    for p in plan:
        p["off"] = off
        off += p["cols"]
    return plan, off


def _img_tiles(block):
    n = block.shape[0] // 128
    w = block.shape[1]
    return np.ascontiguousarray(
        block.reshape(n, 128, w).transpose(1, 0, 2).reshape(128, n * w))


def build_stream(c, plan, total_cols, W):
    out = np.empty((128, total_cols), np.float32)
    for p in plan:
        k, sel, off, cols = p["kind"], p["sel"], p["off"], p["cols"]
        if k in ("upT_row", "gateT_row"):
            kk, h = sel
            srcw = W["upT"] if k == "upT_row" else W["gateT"]
            SH = srcw.shape[1] // 256
            img = srcw[kk * 128:(kk + 1) * 128, h * SH * 128:(h + 1) * SH * 128]
        elif k == "sterm":
            m, o = sel
            src = W["laoT"] if m == 0 else W["fuseT"][m - 1]
            img = _img_tiles(src[:, o * 128:(o + 1) * 128])
        elif k == "fgu":
            gu, f, q = sel
            srcw = W["fgT"] if gu == "g" else W["fuT"]
            FQ = srcw.shape[0] // 512
            img = _img_tiles(srcw[q * FQ * 128:(q + 1) * FQ * 128,
                                  f * 128:(f + 1) * 128])
        elif k == "down":
            si, o, ri, r0, rn = sel
            sec = c["SECS"][si]
            f0 = sec[0] + r0
            img = _img_tiles(W["fdT"][f0 * 128:(f0 + rn) * 128,
                                      o * 128:(o + 1) * 128])
        elif k == "router":
            img = _img_tiles(W["wrT"])
        elif k == "lA":
            q = sel
            FQ = W["lAT"].shape[0] // 512
            img = _img_tiles(W["lAT"][q * FQ * 128:(q + 1) * FQ * 128, :])
        elif k == "b2":
            o, q = sel
            FQ = W["wbT"].shape[0] // 512
            img = _img_tiles(W["wbT"][q * FQ * 128:(q + 1) * FQ * 128,
                                      o * 128:(o + 1) * 128])
        elif k == "lorab":
            q = sel
            qw = W["lBT"].shape[1] // 4
            img = W["lBT"][:, q * qw:(q + 1) * qw]
        else:
            raise KeyError(k)
        assert img.shape == (128, cols), (k, img.shape, cols)
        out[:, off:off + cols] = img
    return out


class Buf:
    __slots__ = ("w", "rs")

    def __init__(self):
        self.w = None
        self.rs = []


class P:
    def __init__(self):
        self.prog = {e: [] for e in ENGS}
        self.cnt = dict.fromkeys(SEMNAMES, 0)
        self.wm = {(e, s): 0 for e in ENGS for s in SEMNAMES}
        self.R = {}

    def wait(self, eng, sem, val):
        if val is None or val <= 0 or self.wm[(eng, sem)] >= val:
            return
        self.wm[(eng, sem)] = val
        self.prog[eng].append(("w", sem, val))

    def waitt(self, eng, tok):
        if tok is not None:
            self.wait(eng, tok[0], tok[1])

    def buf(self, key):
        b = self.R.get(key)
        if b is None:
            b = self.R[key] = Buf()
        return b

    def pre(self, eng, reads=(), writes=()):
        need = {}
        def want(t):
            if t is not None and t[1] > need.get(t[0], 0):
                need[t[0]] = t[1]
        for key in reads:
            want(self.buf(key).w)
        for key in writes:
            b = self.buf(key)
            want(b.w)
            for t in b.rs:
                want(t)
        for sem, val in need.items():
            self.wait(eng, sem, val)

    def post(self, tok, reads=(), writes=(), const_reads=()):
        for key in writes:
            b = self.buf(key)
            b.w = tok
            b.rs = []
        for key in reads:
            self.buf(key).rs.append(tok)

    def emit(self, eng, fn, inc=None, n=1):
        tok = None
        if inc is not None:
            self.cnt[inc] += n
            tok = (inc, self.cnt[inc])
        self.prog[eng].append(("o", fn, inc, n))
        return tok

    def o(self, eng, fn, inc, reads=(), writes=(), n=1):
        self.pre(eng, reads, writes)
        tok = self.emit(eng, fn, inc, n)
        self.post(tok, reads, writes)
        return tok

    def v(self, fn, reads=(), writes=()):
        return self.o("vector", fn, "VS", reads, writes)

    def s(self, fn, reads=(), writes=()):
        return self.o("scalar", fn, "SS", reads, writes)


def build_core_program(cfg):
    c = derive(cfg)
    D, S, FF, NE, R = c["D"], c["S"], c["FF"], c["E"], c["R"]
    KC, SC, FC, OC, TP = c["KC"], c["SC"], c["FC"], c["OC"], c["TP"]
    NPASS, TPC, NT = c["NPASS"], c["TPC"], c["NT"]
    THD, EPS = c["THD"], c["EPS"]
    plan, total_cols = make_plan(c)

    FH = KC // 2
    FQ = KC // 4
    CARVE = {"A": (max(e["cols"] for e in plan), 6)}
    ARENA = max(sc * ns for sc, ns in CARVE.values())

    nc = bass.Bass("TRN2", target_bir_lowering=False)
    xT = nc.dram_tensor("xT", [D, TPC], F32, kind="ExternalInput").ap()
    ws = nc.dram_tensor("wstream", [128, total_cols], F32, kind="ExternalInput").ap()
    lp_h = nc.dram_tensor("lp_pack", [128, SC], F32, kind="ExternalInput").ap()
    b_h = nc.dram_tensor("b_pack", [128, OC], F32, kind="ExternalInput").ap()
    disc_h = nc.dram_tensor("disc_rep", [128, 6], F32, kind="ExternalInput").ap()
    rrep_h = nc.dram_tensor("rrep", [NE, 128], F32, kind="ExternalInput").ap()
    ident_h = nc.dram_tensor("ident", [128, 128], F32, kind="ExternalInput").ap()
    onescol_h = nc.dram_tensor("ones_col", [1, 128], F32, kind="ExternalInput").ap()
    outT = nc.dram_tensor("outT", [D, TPC], F32, kind="ExternalOutput").ap()

    st = contextlib.ExitStack()
    sb = lambda nm, shp, dt: st.enter_context(nc.sbuf_tensor(nm, shp, dt))
    with st:
        acc = sb("acc", [128, OC * TP], F32)
        xbf = sb("xbf", [128, KC * TP], BF16)
        aubf = sb("aubf", [128, SC * TP], BF16)
        delta = sb("delta", [128, SC * TP], F32)
        pw = sb("pw", [128, SC * TP], F32)
        tmpb = [sb("tmp0", [128, SC * TP], BF16), sb("tmp1", [128, SC * TP], BF16)]
        vsec = sb("vsec", [128, c["VSEC"] * TP], BF16)
        wring = sb("wring", [128, ARENA], BF16)
        lp_s = sb("lp_s", [128, SC], F32)
        lpn_s = sb("lpn_s", [128, SC], F32)
        b_s = sb("b_s", [128, OC], F32)
        crep = sb("crep", [128, 6], F32)
        rrep_s = sb("rrep_s", [NE, 128], F32)
        ident_s = sb("ident_s", [128, 128], F32)
        onescol_s = sb("ones_col_s", [1, 128], F32)
        onesk_s = sb("onesk_s", [128, 1], BF16)
        sg = [sb("sgA", [128, TP], F32), sb("sgB", [128, TP], F32)]
        statb = [sb("statA", [128, TP], BF16), sb("statB", [128, TP], BF16)]
        sqb = [sb("sqA", [128, TP], BF16), sb("sqB", [128, TP], BF16)]
        rows = sb("rows", [1, 3 * TP], F32)
        lnt = sb("lnt", [128, TP], F32)
        lt = sb("lt", [128, NT * NE], F32)
        t2a = sb("t2a", [128, NE], F32)
        t2b = sb("t2b", [128, NE], F32)
        t2c = sb("t2c", [128, 4], F32)
        wsb = sb("wsb", [NE, TP], F32)
        wrepsb = sb("wrepsb", [128, TP], F32)
        hbf = sb("hbf", [128, TP], BF16)
        pb = [st.enter_context(nc.psum_tensor("pb%d" % i, [128, TP], F32))
              for i in range(8)]
        SEM = {nm: st.enter_context(nc.semaphore(nm)) for nm in SEMNAMES}
        block = st.enter_context(nc.Block())

        p = P()
        ACCK = [("acc", o) for o in range(OC)]
        XBFK = [("xbf", k) for k in range(KC)]

        plan_by_carve_pos = {}
        cur_carve = [None]
        carve_keys = [set()]

        def dma_chunk(entry, pass_i):
            carve = entry["carve"]
            cols, nslot = CARVE[carve]
            if cur_carve[0] != carve:
                for key in carve_keys[0]:
                    b = p.buf(key)
                    p.waitt("gpsimd", b.w)
                    for t in b.rs:
                        p.waitt("gpsimd", t)
                    del p.R[key]
                cur_carve[0] = carve
                carve_keys[0] = set()
                plan_by_carve_pos[carve] = 0
            i = plan_by_carve_pos[carve]
            plan_by_carve_pos[carve] = i + 1
            slot = i % nslot
            so = slot * cols
            key = ("slot", carve, slot)
            carve_keys[0].add(key)
            ccols = entry["cols"]
            off = entry["off"]
            sem = "W%d" % slot
            tok = p.o("gpsimd",
                      lambda E, off=off, ccols=ccols, so=so:
                      E.dma_start(wring[:, so:so + ccols], ws[:, off:off + ccols]),
                      sem, writes=(key,), n=16)
            return tok, so, key

        def mm_group2(bank, chunks, extra_reads=()):
            p.pre("tensor", reads=(), writes=(("pb", bank),))
            tot = sum(len(it) for _, it in chunks)
            i = 0
            tok = None
            for dret, items in chunks:
                need = {}
                if dret is not None:
                    need[dret[0][0]] = max(need.get(dret[0][0], 0), dret[0][1])
                mov_keys = set()
                for sf, mf, mk in items:
                    if mk is not None:
                        mov_keys.add(mk)
                        t = p.buf(mk).w
                        if t is not None:
                            need[t[0]] = max(need.get(t[0], 0), t[1])
                for sem, val in need.items():
                    p.wait("tensor", sem, val)
                n_it = len(items)
                for j, (sf, mf, mk) in enumerate(items):
                    def fn(E, b=bank, sf=sf, mf=mf, st_=(i == 0), sp=(i == tot - 1)):
                        return E.matmul(pb[b][:, :], sf(), mf(), start=st_, stop=sp)
                    if j == n_it - 1:
                        tok = p.emit("tensor", fn, "TS")
                    else:
                        p.emit("tensor", fn, None)
                    i += 1
                rd = list(mov_keys)
                if dret is not None:
                    rd.append(dret[2])
                p.post(tok, reads=rd, writes=[])
            p.post(tok, reads=list(extra_reads), writes=[("pb", bank)])
            return tok

        const_keys = []
        for ap_s, ap_h, key in ((lp_s, lp_h, "lp"), (b_s, b_h, "b"),
                                (crep, disc_h, "crep"), (rrep_s, rrep_h, "rrep"),
                                (ident_s, ident_h, "ident"),
                                (onescol_s, onescol_h, "onescol")):
            p.emit("sync", lambda E, d=ap_s, s_=ap_h: E.dma_start(d[:, :], s_[:, :]),
                   "CQ", 16)
            const_keys.append(key)
        ctok = ("CQ", p.cnt["CQ"])
        for key in const_keys:
            p.buf(key).w = ctok
        p.s(lambda E: E.activation(crep[:, :], crep[:, :], AF.Exp, scale=-1.0),
            reads=("crep",), writes=("crep",))
        p.v(lambda E: E.tensor_scalar(lpn_s[:, :], lp_s[:, :], -1.0, None, ALU.mult),
            reads=("lp",), writes=("lpn",))
        p.v(lambda E: E.tensor_scalar(onesk_s[:, :], b_s[:, 0:1], 0.0, 1.0,
                                      ALU.mult, ALU.add),
            reads=("b",), writes=("onesk",))

        for ps in range(NPASS):
            t0 = ps * TP
            if ps == 0:
                QO = OC // 4
                for qi, qsem in enumerate(("XF", "XG", "XH", "XI")):
                    p.o("sync", lambda E, t0=t0, qi=qi, QO=QO: E.dma_start(
                        acc.rearrange("p (kc t) -> p kc t", kc=OC)[:, qi * QO:(qi + 1) * QO, :],
                        xT.rearrange("(kc p) t -> p kc t", p=128)[:, qi * QO:(qi + 1) * QO, t0:t0 + TP]),
                        qsem, writes=ACCK[qi * QO:(qi + 1) * QO], n=16)
            else:
                p.o("sync", lambda E, t0=t0: E.dma_start(
                    acc.rearrange("p (kc t) -> p kc t", kc=OC),
                    xT.rearrange("(kc p) t -> p kc t", p=128)[:, :, t0:t0 + TP]),
                    "XQ", writes=ACCK, n=16)
            QK = KC // 4

            def emit_xbf_quarter(qi, t0=t0, QK=QK):
                qsem = ("XB", "XC", "XD", "XE")[qi]
                p.o("gpsimd", lambda E, t0=t0, QK=QK, qi=qi: E.dma_start(
                    xbf.rearrange("p (kc t) -> p kc t", kc=KC)[:, qi * QK:(qi + 1) * QK, :],
                    xT.rearrange("(kc p) t -> p kc t", p=128)[:, qi * QK:(qi + 1) * QK, t0:t0 + TP]),
                    qsem, writes=XBFK[qi * QK:(qi + 1) * QK], n=16)

            plan_i = [0]

            def next_entry(expect_kind=None):
                e = plan[plan_i[0]]
                plan_i[0] += 1
                if expect_kind is not None:
                    assert e["kind"] == expect_kind, (e, expect_kind)
                return e

            SH = SC // 2
            for sub, kind in (("a", "upT_row"), ("b", "gateT_row")):
                for h in range(2):
                    for k in range(KC):
                        if sub == "a" and h == 0 and k % QK == 0 and ps > 0:
                            emit_xbf_quarter(k // QK)
                        if ps == 0 and sub == "a" and h == 0 and k % 2 == 0:
                            for kcc in (k, k + 1):
                                if kcc < KC:
                                    p.s(lambda E, kcc=kcc: E.copy(
                                        xbf[:, kcc * TP:(kcc + 1) * TP],
                                        acc[:, kcc * TP:(kcc + 1) * TP]),
                                        reads=(("acc", kcc),), writes=(("xbf", kcc),))
                        entry = next_entry(kind)
                        assert entry["sel"] == (k, h)
                        wtok, so, skey = dma_chunk(entry, ps)
                        p.waitt("tensor", wtok)
                        for sc in range(SH):
                            if k == 0:
                                p.pre("tensor", writes=(("pb", sc),))
                            p.waitt("tensor", p.buf(("xbf", k)).w)
                            def fn(E, sc=sc, so=so, k=k, st_=(k == 0), sp=(k == KC - 1)):
                                return E.matmul(pb[sc][:, :],
                                                wring[:, so + sc * 128:so + sc * 128 + 128],
                                                xbf[:, k * TP:(k + 1) * TP],
                                                start=st_, stop=sp)
                            if sc == SH - 1:
                                tok = p.emit("tensor", fn, "TS")
                            else:
                                p.emit("tensor", fn, None)
                        p.post(tok, reads=[skey, ("xbf", k)],
                               writes=[("pb", sc2) for sc2 in range(SH)]
                               if k == KC - 1 else [])
                    for sc in range(SH):
                        scg = h * SH + sc
                        if sub == "a":
                            p.s(lambda E, sc=sc, scg=scg: E.copy(
                                aubf[:, scg * TP:(scg + 1) * TP], pb[sc][:, :]),
                                reads=(("pb", sc),), writes=(("aubf", scg),))
                        else:
                            p.s(lambda E, sc=sc, scg=scg: E.activation(
                                delta[:, scg * TP:(scg + 1) * TP], pb[sc][:, :],
                                AF.Identity, bias=lpn_s[:, scg:scg + 1]),
                                reads=(("pb", sc), "lpn"), writes=(("delta", scg),))

            def emit_tmp_chunk(m, sc):
                bi = (m - 1) % 2
                sl = slice(sc * TP, (sc + 1) * TP)
                sgi = sc % 2
                if m == 2:
                    p.v(lambda E, sl=sl: E.tensor_tensor(pw[:, sl], delta[:, sl],
                                                         delta[:, sl], ALU.mult),
                        reads=(("delta", sc),), writes=(("pw", sc),))
                elif m >= 3:
                    p.v(lambda E, sl=sl: E.tensor_tensor(pw[:, sl], pw[:, sl],
                                                         delta[:, sl], ALU.mult),
                        reads=(("pw", sc), ("delta", sc)), writes=(("pw", sc),))
                if m == 5:
                    p.v(lambda E, sl=sl: E.tensor_scalar(lnt[:, :], delta[:, sl],
                                                         THD, None, ALU.is_le),
                        reads=(("delta", sc),), writes=("lnt",))
                    p.v(lambda E, sl=sl: E.tensor_tensor(pw[:, sl], pw[:, sl],
                                                         lnt[:, :], ALU.mult),
                        reads=(("pw", sc), "lnt"), writes=(("pw", sc),))
                    p.v(lambda E, sl=sl: E.tensor_scalar(lnt[:, :], delta[:, sl],
                                                         -THD, None, ALU.is_ge),
                        reads=(("delta", sc),), writes=("lnt",))
                    p.v(lambda E, sl=sl: E.tensor_tensor(pw[:, sl], pw[:, sl],
                                                         lnt[:, :], ALU.mult),
                        reads=(("pw", sc), "lnt"), writes=(("pw", sc),))
                base, bkey = (delta, ("delta", sc)) if m == 1 else (pw, ("pw", sc))
                p.v(lambda E, sl=sl, m=m, base=base, bi=bi: E.scalar_tensor_tensor(
                    tmpb[bi][:, sl], base[:, sl], crep[:, m - 1:m], aubf[:, sl],
                    ALU.mult, ALU.mult),
                    reads=(bkey, "crep", ("aubf", sc)), writes=(("tmp", bi, sc),))

            for m in range(7):
                if m == 0:
                    movbuf, movkey = aubf, lambda kk: ("aubf", kk)
                else:
                    bi = (m - 1) % 2
                    movbuf, movkey = tmpb[bi], (lambda kk, bi=bi: ("tmp", bi, kk))
                for o in range(OC):
                    entry = next_entry("sterm")
                    assert entry["sel"] == (m, o)
                    dret = dma_chunk(entry, ps)
                    so = dret[1]
                    bank = (0, 1, 2, 3)[o % 4]
                    items = [((lambda so=so, kk=kk: wring[:, so + kk * 128:so + kk * 128 + 128]),
                              (lambda kk=kk, movbuf=movbuf: movbuf[:, kk * TP:(kk + 1) * TP]),
                              movkey(kk)) for kk in range(SC)]
                    tok = mm_group2(bank, [(dret, items)])
                    p.v(lambda E, o=o, bank=bank: E.tensor_tensor(
                        acc[:, o * TP:(o + 1) * TP], acc[:, o * TP:(o + 1) * TP],
                        pb[bank][:, :], ALU.add),
                        reads=(("pb", bank), ("acc", o)), writes=(("acc", o),))
                    if m < 6 and o % 2 == 0 and o // 2 < SC:
                        emit_tmp_chunk(m + 1, o // 2)

            fi = 0
            for si, sec in enumerate(c["SECS"]):
                for li, f in enumerate(sec):
                    gb = 2 + 2 * (fi % 2)
                    ub = 3 + 2 * (fi % 2)
                    for gu, bank in (("g", gb), ("u", ub)):
                        chunks = []
                        for q in range(4):
                            e_ = next_entry("fgu")
                            assert e_["sel"] == (gu, f, q)
                            dret = dma_chunk(e_, ps)
                            so = dret[1]
                            items = [((lambda so=so, kk=kk: wring[:, so + kk * 128:so + kk * 128 + 128]),
                                      (lambda kk=kk, q=q: xbf[:, (q * FQ + kk) * TP:(q * FQ + kk + 1) * TP]),
                                      ("xbf", q * FQ + kk)) for kk in range(FQ)]
                            chunks.append((dret, items))
                        mm_group2(bank, chunks)
                    sgi = fi % 2
                    p.s(lambda E, gb=gb, sgi=sgi: E.activation(
                        sg[sgi][:, :], pb[gb][:, :], AF.Silu),
                        reads=(("pb", gb),), writes=(("sg", sgi),))
                    p.v(lambda E, li=li, ub=ub, sgi=sgi: E.tensor_tensor(
                        vsec[:, li * TP:(li + 1) * TP], sg[sgi][:, :],
                        pb[ub][:, :], ALU.mult),
                        reads=(("sg", sgi), ("pb", ub)), writes=(("vsec", li),))
                    fi += 1
                runs = []
                i0 = 0
                while i0 < len(sec):
                    n = min(8, len(sec) - i0)
                    runs.append((i0, n))
                    i0 += n
                for o in range(OC):
                    chunks = []
                    for ri, (r0, rn) in enumerate(runs):
                        entry = next_entry("down")
                        assert entry["sel"] == (si, o, ri, r0, rn)
                        dret = dma_chunk(entry, ps)
                        so = dret[1]
                        items = [((lambda so=so, kk=kk: wring[:, so + kk * 128:so + kk * 128 + 128]),
                                  (lambda kk=kk, r0=r0: vsec[:, (r0 + kk) * TP:(r0 + kk + 1) * TP]),
                                  ("vsec", r0 + kk)) for kk in range(rn)]
                        chunks.append((dret, items))
                    bank = (0, 1, 6, 7)[o % 4]
                    tok = mm_group2(bank, chunks)
                    p.v(lambda E, o=o, bank=bank: E.tensor_tensor(
                        acc[:, o * TP:(o + 1) * TP], acc[:, o * TP:(o + 1) * TP],
                        pb[bank][:, :], ALU.add),
                        reads=(("pb", bank), ("acc", o)), writes=(("acc", o),))

            def layernorm(write_xbf):
                for cc in range(OC):
                    j = cc % 2
                    p.v(lambda E, cc=cc, j=j: E.tensor_copy(
                        statb[j][:, :], acc[:, cc * TP:(cc + 1) * TP]),
                        reads=(("acc", cc),), writes=(("statb", j),))
                    p.s(lambda E, cc=cc, j=j: E.activation(
                        sqb[j][:, :], acc[:, cc * TP:(cc + 1) * TP], AF.Square),
                        reads=(("acc", cc),), writes=(("sqb", j),))
                    p.pre("tensor", reads=(("statb", j), "onesk"),
                          writes=(("pb", 4),) if cc == 0 else ())
                    if cc == 0:
                        p.pre("tensor", writes=(("pb", 5),))
                    def f1(E, j=j, st_=(cc == 0), sp=(cc == OC - 1)):
                        return E.matmul(pb[4][0:1, :], onesk_s[:, :],
                                        statb[j][:, :], start=st_, stop=sp)
                    p.emit("tensor", f1, None)
                    p.pre("tensor", reads=(("sqb", j),))
                    def f2(E, j=j, st_=(cc == 0), sp=(cc == OC - 1)):
                        return E.matmul(pb[5][0:1, :], onesk_s[:, :],
                                        sqb[j][:, :], start=st_, stop=sp)
                    tok = p.emit("tensor", f2, "TS")
                    p.post(tok, reads=[("statb", j), ("sqb", j)],
                           writes=[("pb", 4), ("pb", 5)] if cc == OC - 1 else [])
                r0 = rows[0:1, 0:TP]
                r1 = rows[0:1, TP:2 * TP]
                r2 = rows[0:1, 2 * TP:3 * TP]
                p.v(lambda E: E.tensor_scalar(r0, pb[4][0:1, :], 1.0 / D, None, ALU.mult),
                    reads=(("pb", 4),), writes=("r0",))
                p.v(lambda E: E.tensor_scalar(r1, pb[5][0:1, :], 1.0 / D, None, ALU.mult),
                    reads=(("pb", 5),), writes=("r1",))
                p.v(lambda E: E.tensor_tensor(r2, r0, r0, ALU.mult),
                    reads=("r0",), writes=("r2",))
                p.v(lambda E: E.tensor_tensor(r1, r1, r2, ALU.subtract),
                    reads=("r1", "r2"), writes=("r1",))
                p.v(lambda E: E.tensor_scalar(r1, r1, EPS, None, ALU.add),
                    reads=("r1",), writes=("r1",))
                p.v(lambda E: E.reciprocal(r1, r1), reads=("r1",), writes=("r1",))
                p.s(lambda E: E.activation(r1, r1, AF.Sqrt), reads=("r1",), writes=("r1",))
                p.v(lambda E: E.tensor_tensor(r2, r0, r1, ALU.mult),
                    reads=("r0", "r1"), writes=("r2",))
                p.pre("tensor", reads=("r1", "onescol"), writes=(("pb", 6),))
                p.emit("tensor", lambda E: E.matmul(pb[6][:, :], onescol_s[:, :], r1,
                                                    start=True, stop=True), "TS")
                p.pre("tensor", reads=("r2",), writes=(("pb", 7),))
                tok = p.emit("tensor", lambda E: E.matmul(pb[7][:, :], onescol_s[:, :],
                                                          r2, start=True, stop=True), "TS")
                p.post(tok, reads=["r1", "r2"], writes=[("pb", 6), ("pb", 7)])
                for cc in range(OC):
                    p.v(lambda E, cc=cc: E.tensor_tensor(
                        lnt[:, :], acc[:, cc * TP:(cc + 1) * TP], pb[6][:, :], ALU.mult),
                        reads=(("acc", cc), ("pb", 6)), writes=("lnt",))
                    p.v(lambda E, cc=cc: E.tensor_tensor(
                        acc[:, cc * TP:(cc + 1) * TP], lnt[:, :], pb[7][:, :],
                        ALU.subtract),
                        reads=("lnt", ("pb", 7)), writes=(("acc", cc),))
                    if write_xbf:
                        p.s(lambda E, cc=cc: E.copy(xbf[:, cc * TP:(cc + 1) * TP],
                                                    acc[:, cc * TP:(cc + 1) * TP]),
                            reads=(("acc", cc),), writes=(("xbf", cc),))

            layernorm(write_xbf=True)

            er = next_entry("router")
            wtr, sor, skr = dma_chunk(er, ps)
            p.waitt("tensor", wtr)
            p.pre("tensor", writes=(("pb", 4),))
            for kt in range(KC):
                p.waitt("tensor", p.buf(("xbf", kt)).w)
                def fl(E, kt=kt, sor=sor, st_=(kt == 0), sp=(kt == KC - 1)):
                    return E.matmul(pb[4][0:NE, :], wring[:, sor + kt * 8:sor + kt * 8 + 8],
                                    xbf[:, kt * TP:(kt + 1) * TP], start=st_, stop=sp)
                if kt == KC - 1:
                    tok = p.emit("tensor", fl, "TS")
                else:
                    p.emit("tensor", fl, None)
            p.post(tok, reads=[skr] + XBFK, writes=[("pb", 4)])
            hchunks = []
            for q in range(4):
                ea = next_entry("lA")
                assert ea["sel"] == q
                dret = dma_chunk(ea, ps)
                soa = dret[1]
                items = [((lambda soa=soa, kt=kt: wring[:, soa + kt * 128:soa + kt * 128 + 128]),
                          (lambda kt=kt, q=q: xbf[:, (q * FQ + kt) * TP:(q * FQ + kt + 1) * TP]),
                          ("xbf", q * FQ + kt)) for kt in range(FQ)]
                hchunks.append((dret, items))
            htok = mm_group2(5, hchunks)
            p.v(lambda E: E.tensor_copy(wsb[:, :], pb[4][0:NE, :]),
                reads=(("pb", 4),), writes=("wsb",))
            for j in range(NT):
                p.pre("tensor", reads=("wsb", "ident"),
                      writes=(("pb", 7),) if j == 0 else ())
                def ft(E, j=j):
                    return E.transpose(pb[7][:, j * NE:(j + 1) * NE],
                                       wsb[0:NE, j * 128:(j + 1) * 128],
                                       ident_s[0:NE, 0:NE])
                if j == NT - 1:
                    tok = p.emit("tensor", ft, "TS")
                else:
                    p.emit("tensor", ft, None)
            p.post(tok, reads=["wsb"], writes=[("pb", 7)])
            p.v(lambda E: E.tensor_copy(lt[:, :], pb[7][:, 0:NT * NE]),
                reads=(("pb", 7),), writes=("lt",))
            for j in range(NT):
                ltj = lt[:, j * NE:(j + 1) * NE]
                p.v(lambda E, ltj=ltj: E.reduce_max(t2c[:, 0:1], ltj, axis=AX.X),
                    reads=("lt",), writes=("t2c",))
                p.v(lambda E: E.tensor_scalar(t2c[:, 1:2], t2c[:, 0:1], -1.0, None,
                                              ALU.mult),
                    reads=("t2c",), writes=("t2c",))
                p.v(lambda E, ltj=ltj: E.tensor_scalar(t2a[:, :], ltj, t2c[:, 0:1],
                                                       None, ALU.is_equal),
                    reads=("lt", "t2c"), writes=("t2a",))
                p.v(lambda E, ltj=ltj: E.scalar_tensor_tensor(
                    t2b[:, :], t2a[:, :], -1e30, ltj, ALU.mult, ALU.add),
                    reads=("t2a", "lt"), writes=("t2b",))
                p.v(lambda E: E.reduce_max(t2c[:, 2:3], t2b[:, :], axis=AX.X),
                    reads=("t2b",), writes=("t2c",))
                p.v(lambda E: E.tensor_scalar(t2b[:, :], t2b[:, :], t2c[:, 2:3],
                                              None, ALU.is_equal),
                    reads=("t2b", "t2c"), writes=("t2b",))
                p.v(lambda E: E.tensor_tensor(t2a[:, :], t2a[:, :], t2b[:, :], ALU.add),
                    reads=("t2a", "t2b"), writes=("t2a",))
                p.s(lambda E, ltj=ltj: E.activation(t2b[:, :], ltj, AF.Exp,
                                                    bias=t2c[:, 1:2]),
                    reads=("lt", "t2c"), writes=("t2b",))
                p.v(lambda E: E.tensor_tensor(t2a[:, :], t2b[:, :], t2a[:, :], ALU.mult),
                    reads=("t2b", "t2a"), writes=("t2a",))
                p.v(lambda E: E.reduce_sum(t2c[:, 3:4], t2a[:, :], axis=AX.X),
                    reads=("t2a",), writes=("t2c",))
                p.v(lambda E: E.reciprocal(t2c[:, 3:4], t2c[:, 3:4]),
                    reads=("t2c",), writes=("t2c",))
                p.v(lambda E: E.tensor_scalar(t2c[:, 3:4], t2c[:, 3:4], 2.0, None,
                                              ALU.mult),
                    reads=("t2c",), writes=("t2c",))
                p.v(lambda E, ltj=ltj: E.tensor_scalar(ltj, t2a[:, :], t2c[:, 3:4],
                                                       None, ALU.mult),
                    reads=("t2a", "t2c"), writes=("lt",))
            for o in range(OC):
                chunks = []
                for q in range(4):
                    entry = next_entry("b2")
                    assert entry["sel"] == (o, q)
                    dret = dma_chunk(entry, ps)
                    so = dret[1]
                    items = [((lambda so=so, kt=kt: wring[:, so + kt * 128:so + kt * 128 + 128]),
                              (lambda kt=kt, q=q: xbf[:, (q * FQ + kt) * TP:(q * FQ + kt + 1) * TP]),
                              ("xbf", q * FQ + kt)) for kt in range(FQ)]
                    chunks.append((dret, items))
                bank = (0, 1, 2, 3)[o % 4]
                tok = mm_group2(bank, chunks)
                p.v(lambda E, bank=bank, o=o: E.scalar_tensor_tensor(
                    acc[:, o * TP:(o + 1) * TP], pb[bank][:, :], b_s[:, o:o + 1],
                    acc[:, o * TP:(o + 1) * TP], ALU.add, ALU.add),
                    reads=(("pb", bank), "b", ("acc", o)), writes=(("acc", o),))

            for j in range(NT):
                p.pre("tensor", reads=("lt", "ident"),
                      writes=(("pb", 7),) if j == 0 else ())
                def ft2(E, j=j):
                    return E.transpose(pb[7][0:NE, j * 128:(j + 1) * 128],
                                       lt[:, j * NE:(j + 1) * NE], ident_s[:, :])
                if j == NT - 1:
                    tok = p.emit("tensor", ft2, "TS")
                else:
                    p.emit("tensor", ft2, None)
            p.post(tok, reads=["lt"], writes=[("pb", 7)])
            p.v(lambda E: E.tensor_copy(wsb[:, :], pb[7][0:NE, :]),
                reads=(("pb", 7),), writes=("wsb",))
            p.pre("tensor", reads=("wsb", "rrep"), writes=(("pb", 6),))
            tok = p.emit("tensor", lambda E: E.matmul(pb[6][:, :], rrep_s[:, :],
                                                      wsb[:, :], start=True, stop=True),
                         "TS")
            p.post(tok, reads=["wsb", "rrep"], writes=[("pb", 6)])
            p.s(lambda E: E.copy(wrepsb[:, :], pb[6][:, :]),
                reads=(("pb", 6),), writes=("wrepsb",))
            p.v(lambda E: E.tensor_tensor(hbf[:, :], pb[5][:, :], wrepsb[:, :], ALU.mult),
                reads=(("pb", 5), "wrepsb"), writes=("hbf",))

            QW = OC // 4
            for q in range(4):
                el = next_entry("lorab")
                assert el["sel"] == q
                ldret = dma_chunk(el, ps)
                sol = ldret[1]
                for oo in range(QW):
                    o = q * QW + oo
                    bank = (0, 1, 2, 3)[o % 4]
                    items = [((lambda sol=sol, oo=oo: wring[:, sol + oo * 128:sol + oo * 128 + 128]),
                              (lambda: hbf[:, :]), "hbf")]
                    tok = mm_group2(bank, [(ldret if oo == 0 else None, items)])
                    p.post(tok, reads=[ldret[2]], writes=[])
                    p.v(lambda E, o=o, bank=bank: E.tensor_tensor(
                        acc[:, o * TP:(o + 1) * TP], acc[:, o * TP:(o + 1) * TP],
                        pb[bank][:, :], ALU.add),
                        reads=(("pb", bank), ("acc", o)), writes=(("acc", o),))

            layernorm(write_xbf=False)
            for g0 in range(0, OC, 4):
                p.o("sync", lambda E, t0=t0, g0=g0: E.dma_start(
                    outT.rearrange("(kc p) t -> p kc t", p=128)[:, g0:g0 + 4, t0:t0 + TP],
                    acc.rearrange("p (kc t) -> p kc t", kc=OC)[:, g0:g0 + 4, :]),
                    "XQ", reads=[("acc", cc) for cc in range(g0, g0 + 4)], n=16)
            assert plan_i[0] == len(plan)

        p.wait("sync", "XQ", p.cnt["XQ"])

        def replay(eng):
            def body(E):
                for item in p.prog[eng]:
                    if item[0] == "w":
                        E.wait_ge(SEM[item[1]], item[2])
                    else:
                        _, fn, inc, n = item
                        ins = fn(E)
                        if inc is not None:
                            ins.then_inc(SEM[inc], n)
            return body

        block.tensor(replay("tensor"))
        block.vector(replay("vector"))
        block.scalar(replay("scalar"))
        block.sync(replay("sync"))
        block.gpsimd(replay("gpsimd"))

    return nc, c, plan, total_cols


def stage_weights(c, plan, total_cols, inp):
    W = {
        "upT": np.ascontiguousarray(inp["L0_attn_up_w"].T),
        "gateT": np.ascontiguousarray(inp["L0_attn_gate_w"].T),
        "laoT": np.ascontiguousarray(inp["L0_attn_local_approx_output"].T),
        "fuseT": [np.ascontiguousarray(inp["L0_attn_fuse_weight"][:, :, m].T)
                  for m in range(c["G"])],
        "fgT": np.ascontiguousarray(inp["L0_attn_ffn_gate_w"].T),
        "fuT": np.ascontiguousarray(inp["L0_attn_ffn_up_w"].T),
        "fdT": np.ascontiguousarray(inp["L0_attn_ffn_down_w"].T),
        "wbT": np.ascontiguousarray(inp["L0_ffn_W_base"].T),
        "wrT": np.ascontiguousarray(inp["L0_ffn_W_router"].T),
        "lAT": np.ascontiguousarray(
            inp["L0_ffn_lora_A"].transpose(2, 0, 1).reshape(c["D"], c["E"] * c["R"])),
        "lBT": np.ascontiguousarray(
            inp["L0_ffn_lora_B"].transpose(0, 2, 1).reshape(c["E"] * c["R"], c["D"])),
    }
    stream = build_stream(c, plan, total_cols, W)
    smalls = {
        "lp_pack": np.ascontiguousarray(
            inp["L0_attn_local_point"].reshape(c["SC"], 128).T).astype(np.float32),
        "b_pack": np.ascontiguousarray(
            inp["L0_ffn_b_base"].reshape(c["OC"], 128).T).astype(np.float32),
        "disc_rep": np.ascontiguousarray(
            np.tile(inp["L0_attn_discount_factor"].reshape(1, c["G"]),
                    (128, 1))).astype(np.float32),
        "rrep": None,
        "ident": np.eye(128, dtype=np.float32),
        "ones_col": np.ones((1, 128), np.float32),
    }
    rrep = np.zeros((c["E"], 128), np.float32)
    for e in range(c["E"]):
        rrep[e, e * c["R"]:(e + 1) * c["R"]] = 1.0
    smalls["rrep"] = rrep
    return stream, smalls


def kernel(**inputs):
    inputs = {k: np.asarray(v, dtype=np.float32) for k, v in inputs.items()}
    cfg = real_cfg()
    c = derive(cfg)
    nc, c, plan, total_cols = build_core_program(cfg)
    stream, smalls = stage_weights(c, plan, total_cols, inputs)
    x = inputs["L0_attn_hidden_states"].reshape(c["TOKENS"], c["D"])
    in_maps = []
    for core in range(c["NCORES"]):
        xc = np.ascontiguousarray(
            x[core * c["TPC"]:(core + 1) * c["TPC"], :].T)
        m = {"xT": xc, "wstream": stream}
        m.update(smalls)
        in_maps.append(m)
    res = run_bass_kernel_spmd(nc, in_maps, core_ids=list(range(c["NCORES"])))
    outs = [r["outT"] for r in res.results]
    out = np.concatenate([o.T for o in outs], axis=0)
    return np.ascontiguousarray(
        out.reshape(2, c["TOKENS"] // 2, c["D"])).astype(np.float32)
